# version 08
# baseline (speedup 1.0000x reference)
import numpy as np
from contextlib import ExitStack

import concourse.bass as bass
import concourse.mybir as mybir
from concourse.bass_utils import run_bass_kernel_spmd

F32 = mybir.dt.float32
BF16 = mybir.dt.bfloat16
I32 = mybir.dt.int32
AF = mybir.ActivationFunctionType
ALU = mybir.AluOpType
AX = mybir.AxisListType
PI = float(np.pi)

NB, S, D, NE, FF = 2, 4096, 1024, 8, 4096
QB = 256
NQB = S // QB
NKT = S // 128
NG = 8
BIS_PER_QB = 3

ENGINES = ("tensor", "vector", "scalar", "gpsimd", "sync")
STAGE_LEVELS = {"p0a": 1, "p0": 2, "p1": 3, "p2": 4, "p25": 5, "p3": 6, "full": 99}


class Bld:
    def __init__(self, sems):
        self.streams = {e: [] for e in ENGINES}
        self.sems = sems
        self.n = {k: 0 for k in sems}

    def do(self, eng, fn, inc=None, inck=1, waits=()):
        if inc is not None:
            self.n[inc] += inck
        wl = [(self.sems[s], v) for (s, v) in waits if v]

        def run(e, fn=fn, wl=wl, sem=(self.sems[inc] if inc else None), inck=inck):
            for sh, v in wl:
                e.wait_ge(sh, v)
            instr = fn(e)
            if sem is not None:
                if instr is None:
                    instr = e.nop()
                instr.then_inc(sem, inck)
        self.streams[eng].append(run)
        return self.n[inc] if inc else None

    def wait(self, eng, sem, val):
        sh = self.sems[sem]
        self.streams[eng].append(lambda e: e.wait_ge(sh, val))

    def cnt(self, sem):
        return self.n[sem]

    def barrier(self):
        for s in ("dx", "dw", "dg"):
            self.wait("sync", s, self.n[s])
        for e in ENGINES:
            self.do(e, lambda en: en.nop(), inc="bar")
        v = self.n["bar"]
        for e in ENGINES:
            self.wait(e, "bar", v)


def build_program(stage="full"):
    import os
    DBGB = int(os.environ.get("KDBG_BATCH", "0"))
    if stage.startswith("b1"):
        DBGB = 1
        stage = stage[2:]
    LVL = STAGE_LEVELS[stage]
    DBG = True

    nc = bass.Bass("TRN2", target_bir_lowering=False)

    x_in = nc.dram_tensor("x", [NB * S, D], F32, kind="ExternalInput")
    wq = nc.dram_tensor("wq", [D, D], F32, kind="ExternalInput")
    wk = nc.dram_tensor("wk", [D, D], F32, kind="ExternalInput")
    wv = nc.dram_tensor("wv", [D, D], F32, kind="ExternalInput")
    wo = nc.dram_tensor("wo", [D, D], F32, kind="ExternalInput")
    bq_d = nc.dram_tensor("bq", [1, D], F32, kind="ExternalInput")
    bk_d = nc.dram_tensor("bk", [1, D], F32, kind="ExternalInput")
    bv_d = nc.dram_tensor("bv", [1, D], F32, kind="ExternalInput")
    bo_d = nc.dram_tensor("bo", [1, D], F32, kind="ExternalInput")
    wg_d = nc.dram_tensor("wg", [NE, D], F32, kind="ExternalInput")
    w1_d = nc.dram_tensor("w1", [NE * D, FF], F32, kind="ExternalInput")
    b1_d = nc.dram_tensor("b1", [NE, FF], F32, kind="ExternalInput")
    w2_d = nc.dram_tensor("w2", [NE * FF, D], F32, kind="ExternalInput")
    b2_d = nc.dram_tensor("b2", [NE, D], F32, kind="ExternalInput")
    y_out = nc.dram_tensor("y", [NB * S, D], F32, kind="ExternalOutput")

    xn_s = nc.dram_tensor("xn_s", [S, D], BF16)
    qts = nc.dram_tensor("qts", [D, S], BF16)
    attnT_s = nc.dram_tensor("attnT_s", [D, S], BF16)
    yattn_s = nc.dram_tensor("yattn_s", [S, D], BF16)
    acc_s = nc.dram_tensor("acc_s", [S, D], F32)
    fq_s = nc.dram_tensor("fq_s", [1, S], BF16)
    rs_s = nc.dram_tensor("rs_s", [1, S], BF16)
    fence_s = nc.dram_tensor("fence_s", [128, 4], F32)
    wqbf = nc.dram_tensor("wqbf", [D, D], BF16)
    wkbf = nc.dram_tensor("wkbf", [D, D], BF16)
    wvbf = nc.dram_tensor("wvbf", [D, D], BF16)
    wobf = nc.dram_tensor("wobf", [D, D], BF16)

    dbg = {}
    if DBG:
        def dbgt(nm, sh, dt):
            dbg[nm] = nc.dram_tensor(nm, sh, dt, kind="ExternalOutput")
        dbgt("d_xn", [S, D], BF16); dbgt("d_sm", [NE, S], BF16)
        dbgt("d_kt", [D, S], BF16); dbgt("d_v", [S, D], BF16)
        dbgt("d_ick", [128, 32], F32)
        dbgt("d_qt", [D, S], BF16); dbgt("d_fq", [1, S], BF16)
        dbgt("d_cq", [1, 64], F32)
        dbgt("d_at", [D, S], BF16); dbgt("d_rs", [1, S], BF16)
        dbgt("d_ya", [S, D], BF16); dbgt("d_ic", [128, 32], F32)
        dbgt("d_gc", [128, 32], F32); dbgt("d_mr", [128, 256], F32)
        dbgt("d_acc", [S, D], F32)
        dbgt("d_c8", [128, 8], F32)
        dbgt("d_st", [128, 12], F32)

    eye_np = np.eye(128, dtype=np.float32)
    vones_np = np.zeros((128, 128), dtype=np.float32); vones_np[:, 0] = 1.0
    ones1_np = np.ones((1, 128), dtype=np.float32)
    hilo_np = np.zeros((128, 96 * NKT), dtype=np.float32)
    for t in range(NKT):
        toks = t * 128 + np.arange(128)
        hilo_np[:, 96 * t] = toks // 256
        hilo_np[:, 96 * t + 32] = toks % 256
    h_eye = nc.inline_tensor(eye_np, "c_eye")
    h_vones = nc.inline_tensor(vones_np, "c_vones")
    h_ones1 = nc.inline_tensor(ones1_np, "c_ones1")
    h_hilo = nc.inline_tensor(hilo_np, "c_hilo")

    es = ExitStack()
    sbt = lambda nm, sh, dt: es.enter_context(nc.sbuf_tensor(nm, sh, dt))
    pst = lambda nm, sh, dt: es.enter_context(nc.psum_tensor(nm, sh, dt))

    AB0 = sbt("AB0", [128, 32768], BF16)
    AB1 = sbt("AB1", [128, 32768], BF16)
    AB2 = sbt("AB2", [128, 20480], BF16)
    AB3 = sbt("AB3", [128, 8192], BF16)
    AFa = sbt("AFa", [128, 3072], F32)
    AT = sbt("AT", [128, 2304], BF16)

    eyef = sbt("eyef", [128, 128], F32)
    eyeb = sbt("eyeb", [128, 128], BF16)
    vones = sbt("vones", [128, 128], BF16)
    ones1 = sbt("ones1", [1, 128], BF16)
    onesc = sbt("onesc", [128, 1], BF16)
    fqsl = sbt("fqsl", [1, 512], BF16)
    fqbb = sbt("fqbb", [128, 256], BF16)
    smt_bf = sbt("smt_bf", [128, 256], BF16)
    b1c = sbt("b1c", [128, 256], F32)
    wgt = sbt("wgt", [128, 64], BF16)
    bqc = sbt("bqc", [128, 8], F32)
    bkc = sbt("bkc", [128, 8], F32)
    ickc = sbt("ickc", [128, 32], F32)
    irsc = sbt("irsc", [128, 32], F32)
    icol_all = sbt("icol_all", [128, 32], I32)
    gcol_all = sbt("gcol_all", [128, 32], F32)
    bis = sbt("bis", [8, 32], F32)
    bscr = sbt("bscr", [8, 512], BF16)
    c8 = sbt("c8", [128, 8], F32)
    cq = sbt("cq", [1, 64], F32)
    cqb_bf = sbt("cqb_bf", [1, 32], BF16)
    probst = sbt("probst", [128, 8], F32)
    stats = sbt("stats", [128, 12], F32)
    mxk = sbt("mxk", [1, 8], F32)

    pb = [pst("pb0", [128, 512], F32), pst("pb1", [128, 512], F32)]
    pa = [pst(f"pa{i}", [128, 512], F32) for i in range(4)]
    pr = pst("pr", [128, 512], F32)
    pt = pst("pt", [128, 1024], BF16)

    sems = {}
    for s_name in ("dx", "dw", "dg", "pe", "ve", "se", "ge", "bar"):
        sems[s_name] = es.enter_context(nc.semaphore(s_name))
    b = Bld(sems)
    bc_reg = nc.gpsimd.alloc_register("bc4095")
    b.do("gpsimd", lambda e: e.reg_mov(bc_reg, S - 1) or e.nop())

    Kt = lambda i: AB0[:, i * 4096:(i + 1) * 4096]
    Vt = lambda t: AB1[:, t * 1024:(t + 1) * 1024]
    W1t = lambda h: AB0[:, h * 4096:(h + 1) * 4096]
    W2t = lambda f: AB1[:, f * 1024:(f + 1) * 1024]
    WoTt = lambda d_: AB1[:, d_ * 1024:(d_ + 1) * 1024]
    xnT_g = lambda h: AB2[:, h * 512:(h + 1) * 512]
    sqb = [AB2[:, 4096:4608], AB2[:, 4608:5120]]
    xn_slot = [AB2[:, 5120:6144], AB2[:, 6144:7168]]
    qtile = [AB2[:, 7168:7680], AB2[:, 7680:8192]]
    sqd = AB2[:, 8192:9216]
    wproj = lambda h: AB2[:, 10240 + h * 1024:10240 + (h + 1) * 1024]
    Abuf = lambda kt: AB2[:, kt * 256:(kt + 1) * 256]
    Bbuf = lambda kt: AB2[:, 8192 + kt * 256:8192 + (kt + 1) * 256]
    qsl = [AB2[:, 16384:18432], AB2[:, 18432:20480]]
    eqm = AB2[0:NE, 0:4096]
    gtm = AB2[0:NE, 4096:8192]
    ab2f = AB2[:, 0:4096].bitcast(F32)
    jiota = ab2f[:, 0:512]
    igs = ab2f[0:96, 512:1024]
    selT = [AB2[:, 4096:4608], AB2[:, 4608:5120]]
    packed = AB2[:, 5120:8192]
    packed3 = packed.rearrange("p (t c) -> p t c", c=96)
    rsld = AB2[0:1, 8192:12288]
    asl = lambda d_, s_: AB2[:, 12288 + (s_ * 8 + d_) * 256:12288 + (s_ * 8 + d_ + 1) * 256]
    yat = [AB2[:, 16384:17408], AB2[:, 17408:18432]]
    h1T = lambda f: AB2[:, f * 512:(f + 1) * 512]
    exTt = lambda h: AB3[:, h * 512:(h + 1) * 512]
    exbuf = [AB3[:, 4096:5120], AB3[:, 5120:6144]]
    b2row = AB3[0:1, 6144:7168]
    Sm8 = AB3[0:NE, 0:4096]
    yld = [AB2[:, 0:1024], AB2[:, 1024:2048]]
    p4f = AB2[:, 2048:8192].bitcast(F32)
    xa4 = [AFa[:, 0:1024], p4f[:, 0:1024]]
    ac4 = [AFa[:, 1024:2048], p4f[:, 1024:2048]]
    cv4 = [AFa[:, 2048:3072], p4f[:, 2048:3072]]
    xstage = [AFa[:, 0:1024], AFa[:, 1024:2048]]
    zerot = AFa[:, 2048:2560]
    nrow = AFa[0:1, 0:1024]
    out_e = [AFa[:, 0:1024], AFa[:, 1024:2048]]
    i_row = AFa[0:1, 0:512]
    g_row = AFa[0:1, 512:1024]
    prep_row = AFa[0:32, 0:128]
    atile = lambda d_: AT[:, d_ * 256:(d_ + 1) * 256]
    bv_row = AT[0:1, 0:1024]
    bo_row = AT[0:1, 1024:2048]
    mrankT = AT[:, 0:512].bitcast(F32)
    th = bis[:, 0:1]; lo_ = bis[:, 1:2]; hi_ = bis[:, 2:3]
    cntc = bis[:, 3:4]; mvec = bis[:, 4:5]; mivec = bis[:, 5:6]
    dvec = bis[:, 6:7]; needc = bis[:, 7:8]; ngt = bis[:, 16:17]
    ccols = bis[0:8, 8:16]
    halfpi = c8[:, 0:1]; epsc = c8[:, 1:2]
    cneg = [c8[:, 2:3], c8[:, 3:4]]
    inv32c = c8[:, 4:5]
    s_sum = stats[:, 0:1]; s_sq = stats[:, 1:2]
    s_mean = stats[:, 3:4]; s_msq = stats[:, 4:5]; s_var = stats[:, 5:6]
    s_rstd = stats[:, 6:7]; s_nmr = stats[:, 7:8]; s_rcp = stats[:, 8:9]

    def mm(ap, lhsT, rhs, start, stop, inc=None, waits=()):
        return b.do("tensor",
                    lambda e, a=ap, l=lhsT, r=rhs, s1=start, s2=stop:
                    e.matmul(a, l, r, start=s1, stop=s2), inc=inc, waits=waits)

    def tp(ap, in_ap, ident, inc=None, waits=(), bp=0):
        np_ = in_ap.shape[0]
        ident = ident[bp:bp + np_, bp:bp + np_]
        return b.do("tensor",
                    lambda e, a=ap, i=in_ap, d=ident: e.transpose(a, i, d),
                    inc=inc, waits=waits)

    def act(out_ap, in_ap, func, scale=1.0, bias=0.0, accum=None, inc="se", waits=()):
        return b.do("scalar",
                    lambda e, o=out_ap, i=in_ap, f=func, s=scale, bi=bias, a=accum:
                    e.activation(out=o, in_=i, func=f, scale=s, bias=bi, accum_out=a),
                    inc=inc, waits=waits)

    def vop(fn, inc="ve", waits=()):
        return b.do("vector", fn, inc=inc, waits=waits)

    def vch(fn, waits=()):
        return b.do("vector", fn, inc="ve", waits=[("ve", b.cnt("ve"))] + list(waits))

    def dma(eng, out_ap, in_ap, inc, waits=(), transpose=False):
        if transpose:
            f = lambda e, o=out_ap, i=in_ap: e.dma_start(out=o, in_=i, transpose=True)
        else:
            f = lambda e, o=out_ap, i=in_ap: e.dma_start(out=o, in_=i)
        return b.do(eng, f, inc=inc, inck=16, waits=waits)

    def gfence(waits=()):
        return b.do("gpsimd", lambda e: e.dma_start(out=fence_s[:, :], in_=eyef[:, 0:4]),
                    inc="dg", inck=16, waits=waits)

    def load_wtile_128x(eng, out_ap, dram, row0, nrows, inc, waits=(), ncols=D):
        ntile = nrows // 128
        ap = bass.AP(dram, row0 * ncols, [[ncols, 128], [128 * ncols, ntile], [1, ncols]])
        return dma(eng, out_ap, ap, inc, waits=waits)

    dma("sync", eyef[:, :], h_eye[:, :], "dx")
    dma("gpsimd", eyeb[:, :], h_eye[:, :], "dg")
    dma("gpsimd", vones[:, :], h_vones[:, :], "dg")
    dma("gpsimd", ones1[:, :], h_ones1[:, :], "dg")
    dg_consts = b.cnt("dg")
    vop(lambda e: e.memset(halfpi, PI / 2))
    vop(lambda e: e.memset(epsc, 1e-5))
    vop(lambda e: e.memset(inv32c, 1.0 / 32))
    vop(lambda e: e.memset(onesc[:, :], 1.0))
    vop(lambda e: e.memset(zerot, 0.0))

    cast_jobs = [(wq, wqbf, i) for i in range(8)] + [(wk, wkbf, i) for i in range(8)] + \
                [(wv, wvbf, i) for i in range(8)] + [(wo, wobf, i) for i in range(8)]
    for idx, (w_i, w_b, i) in enumerate(cast_jobs):
        slot = AB1[:, (idx % 8) * 1024:(idx % 8 + 1) * 1024]
        waits = [("dw", 16 * (idx - 7))] if idx >= 8 else []
        b.do("gpsimd",
             lambda e, sl=slot, w=w_i, ii=i: e.dma_start(out=sl, in_=w[ii * 128:(ii + 1) * 128, :]),
             inc="dg", inck=16, waits=waits)
        gv = b.cnt("dg")
        b.do("scalar",
             lambda e, sl=slot, w=w_b, ii=i: e.dma_start(out=w[ii * 128:(ii + 1) * 128, :], in_=sl),
             inc="dw", inck=16, waits=[("dg", gv)])

    dma("sync", AFa[0:8, 0:1024], wg_d[:, :], "dx")
    dxv = b.cnt("dx")
    vop(lambda e: e.tensor_copy(out=AB3[0:8, 0:1024], in_=AFa[0:8, 0:1024]),
        waits=[("dx", dxv), ("dg", dg_consts)])
    vv = b.cnt("ve")
    for h in range(8):
        tp(pt[0:128, h * 8:(h + 1) * 8], AB3[0:8, h * 128:(h + 1) * 128], eyeb[:, :],
           inc="pe" if h == 7 else None, waits=([("ve", vv)] if h == 0 else ()))
    pv = b.cnt("pe")
    b.do("scalar", lambda e: e.copy(out=wgt[:, :], in_=pt[0:128, 0:64]), inc="se",
         waits=[("pe", pv)])

    for (bd, bc, fac) in ((bq_d, bqc, 1.0 / 32), (bk_d, bkc, 1.0)):
        dma("sync", AFa[0:8, 0:128], bd[0:1, :].rearrange("a (c d) -> (a c) d", d=128), "dx",
            waits=[("ve", b.cnt("ve")), ("se", b.cnt("se"))])
        dxv = b.cnt("dx")
        tp(pr[0:128, 0:8], AFa[0:8, 0:128], eyef[:, :], inc="pe", waits=[("dx", dxv)])
        pv = b.cnt("pe")
        vop(lambda e, bc=bc: e.tensor_copy(out=bc[:, :], in_=pr[0:128, 0:8]),
            waits=[("pe", pv)])
        if fac != 1.0:
            vch(lambda e, bc=bc, f=fac: e.tensor_scalar_mul(bc[:, :], bc[:, :], f))
        b.wait("tensor", "ve", b.cnt("ve"))
    for ei in range(NE):
        dma("sync", prep_row, b1_d[ei:ei + 1, :].rearrange("a (c d) -> (a c) d", d=128), "dx",
            waits=[("ve", b.cnt("ve"))])
        dxv = b.cnt("dx")
        tp(pr[0:128, 8:40], prep_row, eyef[:, :], inc="pe", waits=[("dx", dxv)])
        pv = b.cnt("pe")
        vop(lambda e, ei=ei: e.tensor_copy(out=b1c[:, ei * 32:(ei + 1) * 32], in_=pr[0:128, 8:40]),
            waits=[("pe", pv)])
        b.wait("tensor", "ve", b.cnt("ve"))
    dma("sync", AFa[0:1, 0:1024], bv_d[0:1, :], "dx", waits=[("ve", b.cnt("ve"))])
    vop(lambda e: e.tensor_copy(out=bv_row, in_=AFa[0:1, 0:1024]), waits=[("dx", b.cnt("dx"))])

    b.barrier()

    def phase_p0a(bi):
        xoff = bi * S
        vop(lambda e: e.memset(mxk[0:1, 0:8], 0.0))
        vop(lambda e: e.memset(zerot, 0.0))
        se_xn = {}
        dw_xn = {}
        logit_v = 0
        logit_s = 0
        ve_smdrain = 0
        pe_xnT = 0
        for g in range(NG):
            for tc in range(4):
                t_abs = g * 4 + tc
                sl = t_abs % 2
                waits = [("se", se_xn[t_abs - 2])] if t_abs >= 2 else []
                dma("sync", xstage[sl], x_in[xoff + t_abs * 128:xoff + (t_abs + 1) * 128, :],
                    "dx", waits=waits)
                dxv = b.cnt("dx")
                vop(lambda e, sl=sl: e.reduce_sum(out=s_sum, in_=xstage[sl], axis=AX.X),
                    waits=[("dx", dxv)])
                act(sqd, xstage[sl], AF.Square, accum=s_sq, waits=[("dx", dxv)])
                sev = b.cnt("se")
                vch(lambda e: e.tensor_scalar_mul(s_mean, s_sum, 1.0 / D))
                vch(lambda e: e.tensor_mul(s_msq, s_mean, s_mean))
                vch(lambda e: e.scalar_tensor_tensor(
                    out=s_var, in0=s_sq, scalar=1.0 / D, in1=s_msq,
                    op0=ALU.mult, op1=ALU.subtract), waits=[("se", sev)])
                vev = b.cnt("ve")
                act(s_var, s_var, AF.Ln, bias=epsc, waits=[("ve", vev)])
                act(s_rstd, s_var, AF.Exp, scale=-0.5)
                sev = b.cnt("se")
                vch(lambda e: e.scalar_tensor_tensor(
                    out=s_nmr, in0=s_mean, scalar=-1.0, in1=s_rstd,
                    op0=ALU.mult, op1=ALU.mult), waits=[("se", sev)])
                vev = b.cnt("ve")
                wxn = [("ve", vev)]
                if t_abs >= 2:
                    wxn.append(("dw", dw_xn[t_abs - 2]))
                act(xn_slot[sl], xstage[sl], AF.Identity, scale=s_rstd, bias=s_nmr,
                    waits=wxn)
                se_xn[t_abs] = b.cnt("se")
                b.do("scalar",
                     lambda e, sl=sl, t=t_abs, xo=0: e.dma_start(
                         out=xn_s[t * 128:(t + 1) * 128, :], in_=xn_slot[sl]),
                     inc="dw", inck=16)
                dw_xn[t_abs] = b.cnt("dw")
            for h in range(8):
                w_ = [("dw", dw_xn[g * 4 + 3]), ("pe", pe_xnT)] if h == 0 else ()
                dma("sync", xnT_g(h), xn_s[g * 512:(g + 1) * 512, h * 128:(h + 1) * 128],
                    "dx", waits=w_, transpose=True)
            dxT = b.cnt("dx")
            for tc in range(4):
                t_abs = g * 4 + tc
                for h in range(8):
                    w_ = []
                    if h == 0:
                        w_ = [("dx", dxT), ("ve", logit_v), ("se", logit_s)]
                    mm(pa[0][:, 0:8], xnT_g(h)[:, tc * 128:(tc + 1) * 128],
                       wgt[:, h * 8:(h + 1) * 8], h == 0, h == 7,
                       inc="pe" if h == 7 else None, waits=w_)
                pv = b.cnt("pe")
                vop(lambda e: e.reduce_max(out=s_sum, in_=pa[0][:, 0:8], axis=AX.X, negate=True),
                    waits=[("pe", pv)])
                vev = b.cnt("ve")
                act(probst[:, :], pa[0][:, 0:8], AF.Exp, bias=s_sum, waits=[("ve", vev)])
                logit_s = b.cnt("se")
                vop(lambda e: e.reduce_sum(out=s_msq, in_=probst[:, :], axis=AX.X),
                    waits=[("se", logit_s)])
                vch(lambda e: e.reciprocal(out=s_rcp, in_=s_msq))
                vch(lambda e, t=t_abs: e.tensor_scalar(
                    out=smt_bf[:, t * 8:(t + 1) * 8], in0=probst[:, :],
                    scalar1=s_rcp, scalar2=None, op0=ALU.mult))
                logit_v = b.cnt("ve")
                tp(pt[0:8, tc * 128:(tc + 1) * 128], smt_bf[:, t_abs * 8:(t_abs + 1) * 8],
                   eyeb[:, :], inc="pe",
                   waits=[("ve", logit_v)] + ([("ve", ve_smdrain)] if tc == 0 else []))
            pv = b.cnt("pe")
            pe_xnT = pv
            vop(lambda e, g=g: e.tensor_copy(out=Sm8[:, g * 512:(g + 1) * 512],
                                             in_=pt[0:8, 0:512]),
                waits=[("pe", pv)])
            ve_smdrain = b.cnt("ve")
        if DBG and bi == DBGB:
            dma("gpsimd", dbg["d_xn"][:, :], xn_s[:, :], "dg", waits=[("dw", dw_xn[31])])
            b.do("scalar", lambda e: e.dma_start(out=dbg["d_sm"][:, :], in_=Sm8[:, :]),
                 inc="dw", inck=16, waits=[("ve", b.cnt("ve"))])
            b.do("scalar", lambda e: e.dma_start(out=dbg["d_c8"][:, :], in_=c8[:, :]),
                 inc="dw", inck=16)
            b.do("scalar", lambda e: e.dma_start(out=dbg["d_st"][:, :], in_=stats[:, :]),
                 inc="dw", inck=16)

    def phase_proj_kv(bi):
        for h in range(8):
            dma("sync", wproj(h), wkbf[0:1024, h * 128:(h + 1) * 128], "dx", transpose=True)
        dxW = b.cnt("dx")
        ve_ck = 0
        pe_ck = 0
        pe_xnT = 0
        se_kd = {}
        for g in range(NG):
            for h in range(8):
                w_ = [("pe", pe_xnT)] if h == 0 else ()
                dma("sync", xnT_g(h), xn_s[g * 512:(g + 1) * 512, h * 128:(h + 1) * 128],
                    "dx", transpose=True, waits=w_)
            dxT = b.cnt("dx")
            for i in range(8):
                bank = pb[i % 2]
                for h in range(8):
                    w_ = []
                    if h == 0:
                        w_ = [("dx", max(dxT, dxW))]
                        prev = se_kd.get((g, i - 2)) if i >= 2 else se_kd.get((g - 1, i + 6))
                        if prev:
                            w_.append(("se", prev))
                    mm(bank[:, 0:512], wproj(h)[:, i * 128:(i + 1) * 128], xnT_g(h),
                       h == 0, h == 7, inc="pe" if h == 7 else None, waits=w_)
                pv = b.cnt("pe")
                if i == 7:
                    pe_xnT = pv
                act(Kt(i)[:, g * 512:(g + 1) * 512], bank[:, 0:512], AF.Identity,
                    bias=bkc[:, i:i + 1], waits=[("pe", pv)])
                act(sqb[i % 2], Kt(i)[:, g * 512:(g + 1) * 512], AF.Square,
                    waits=([("pe", pe_ck)] if (i >= 2 or g > 0) and pe_ck else []))
                se_kd[(g, i)] = b.cnt("se")
                sev = b.cnt("se")
                pe_ck = mm(pr[0:1, 0:512], onesc[:, :], sqb[i % 2], i == 0, i == 7,
                           inc="pe",
                           waits=[("se", sev)] + ([("ve", ve_ck)] if i == 0 and g > 0 else []))
            pv = b.cnt("pe")
            vop(lambda e: e.tensor_copy(out=nrow[0:1, 0:512], in_=pr[0:1, 0:512]),
                waits=[("pe", pv)])
            vev = b.cnt("ve")
            act(nrow[0:1, 0:512], nrow[0:1, 0:512], AF.Ln, waits=[("ve", vev)])
            act(nrow[0:1, 0:512], nrow[0:1, 0:512], AF.Exp, scale=0.5)
            sev = b.cnt("se")
            vop(lambda e: e.tensor_scalar_add(nrow[0:1, 0:512], nrow[0:1, 0:512], 1e-8),
                waits=[("se", sev)])
            vch(lambda e: e.reduce_max(out=mxk[0:1, 1:2], in_=nrow[0:1, 0:512], axis=AX.X))
            vch(lambda e: e.tensor_max(mxk[0:1, 0:1], mxk[0:1, 0:1], mxk[0:1, 1:2]))
            vev = b.cnt("ve")
            for c in range(4):
                tp(pr[0:128, c:c + 1], nrow[0:1, c * 128:(c + 1) * 128], eyef[:, :],
                   inc="pe" if c == 3 else None, waits=([("ve", vev)] if c == 0 else ()))
            pv = b.cnt("pe")
            vop(lambda e, g=g: e.tensor_copy(out=ickc[:, g * 4:(g + 1) * 4], in_=pr[0:128, 0:4]),
                waits=[("pe", pv)])
            ve_ck = b.cnt("ve")
        vch(lambda e: e.reciprocal(out=gcol_all[:, :], in_=ickc[:, :]))
        vch(lambda e: e.tensor_scalar_mul(ickc[:, :], gcol_all[:, :], -1.0))
        b.barrier()
        for h in range(8):
            dma("sync", wproj(h), wvbf[0:1024, h * 128:(h + 1) * 128], "dx", transpose=True)
        dma("sync", AFa[0:1, 1024:2048], bv_d[0:1, :], "dx")
        dxbv = b.cnt("dx")
        vop(lambda e: e.tensor_copy(out=bv_row, in_=AFa[0:1, 1024:2048]),
            waits=[("dx", dxbv)])
        ve_bv = b.cnt("ve")
        b.wait("tensor", "ve", ve_bv)
        dxW = b.cnt("dx")
        se_vd = {}
        pe_xnT = 0
        for g in range(NG):
            for h in range(8):
                w_ = [("pe", pe_xnT)] if h == 0 else ()
                dma("sync", xnT_g(h), xn_s[g * 512:(g + 1) * 512, h * 128:(h + 1) * 128],
                    "dx", transpose=True, waits=w_)
            dxT = b.cnt("dx")
            for tc in range(4):
                for hf in range(2):
                    k_idx = tc * 2 + hf
                    bank = pb[k_idx % 2]
                    for h in range(8):
                        w_ = []
                        if h == 0:
                            w_ = [("dx", max(dxT, dxW))]
                            prev = se_vd.get((g, k_idx - 2)) if k_idx >= 2 else se_vd.get((g - 1, k_idx + 6))
                            if prev:
                                w_.append(("se", prev))
                        mm(bank[:, 0:512], xnT_g(h)[:, tc * 128:(tc + 1) * 128],
                           wproj(h)[:, hf * 512:(hf + 1) * 512], h == 0, False, waits=w_)
                    mm(bank[:, 0:512], ones1[0:1, :], bv_row[0:1, hf * 512:(hf + 1) * 512],
                       False, True, inc="pe")
                    pv = b.cnt("pe")
                    if k_idx == 7:
                        pe_xnT = pv
                    act(Vt(g * 4 + tc)[:, hf * 512:(hf + 1) * 512], bank[:, 0:512],
                        AF.Copy, waits=[("pe", pv)])
                    se_vd[(g, k_idx)] = b.cnt("se")
        if DBG and bi == DBGB:
            b.wait("scalar", "se", b.cnt("se"))
            for i in range(8):
                b.do("scalar", lambda e, i=i: e.dma_start(
                    out=dbg["d_kt"][i * 128:(i + 1) * 128, :], in_=Kt(i)), inc="dw", inck=16)
            for t in range(32):
                b.do("scalar", lambda e, t=t: e.dma_start(
                    out=dbg["d_v"][t * 128:(t + 1) * 128, :], in_=Vt(t)), inc="dw", inck=16)
            b.do("scalar", lambda e: e.dma_start(out=dbg["d_ick"][:, :], in_=ickc[:, :]),
                 inc="dw", inck=16, waits=[("ve", b.cnt("ve"))])

    def phase_p1(bi):
        for h in range(8):
            dma("sync", wproj(h), wqbf[0:1024, h * 128:(h + 1) * 128], "dx", transpose=True)
        dxW = b.cnt("dx")
        dw_qt = {}
        dw_fq = {}
        sq_se = {}
        pe_nq = 0
        ve_nq = 0
        pe_xnT = 0
        for g in range(NG):
            for h in range(8):
                w_ = [("pe", pe_xnT)] if h == 0 else ()
                dma("sync", xnT_g(h), xn_s[g * 512:(g + 1) * 512, h * 128:(h + 1) * 128],
                    "dx", transpose=True, waits=w_)
            dxT = b.cnt("dx")
            for i in range(8):
                bank = pb[i % 2]
                for h in range(8):
                    w_ = []
                    if h == 0:
                        w_ = [("dx", max(dxT, dxW))]
                        pdw = dw_qt.get((g, i - 2)) if i >= 2 else dw_qt.get((g - 1, i + 6))
                        if pdw:
                            w_.append(("dw", pdw))
                    mm(bank[:, 0:512], wproj(h)[:, i * 128:(i + 1) * 128], xnT_g(h),
                       h == 0, h == 7, inc="pe" if h == 7 else None, waits=w_)
                pv = b.cnt("pe")
                if i == 7:
                    pe_xnT = pv
                wq_ = [("pe", pv)]
                if sq_se.get(i - 2 if i >= 2 else (None if g == 0 else i + 6)):
                    pass
                prev_sq = sq_se.get((g, i - 2)) if i >= 2 else sq_se.get((g - 1, i + 6))
                if prev_sq:
                    wq_.append(("se", prev_sq))
                vop(lambda e, i=i, bank=bank, sl=i % 2: e.tensor_scalar(
                    out=qtile[sl], in0=bank[:, 0:512], scalar1=inv32c,
                    scalar2=bqc[:, i:i + 1], op0=ALU.mult, op1=ALU.add),
                    waits=wq_)
                vev_q = b.cnt("ve")
                b.do("scalar",
                     lambda e, i=i, g=g, sl=i % 2: e.dma_start(
                         out=qts[i * 128:(i + 1) * 128, g * 512:(g + 1) * 512], in_=qtile[sl]),
                     inc="dw", inck=16, waits=[("ve", vev_q)])
                dw_qt[(g, i)] = b.cnt("dw")
                act(sqb[i % 2], qtile[i % 2], AF.Square,
                    waits=[("ve", vev_q)] + ([("pe", pe_nq)] if (i >= 2 or g > 0) and pe_nq else []))
                sq_se[(g, i)] = b.cnt("se")
                sev = b.cnt("se")
                pe_nq = mm(pr[0:1, 0:512], onesc[:, :], sqb[i % 2], i == 0, i == 7,
                           inc="pe",
                           waits=[("se", sev)] + ([("ve", ve_nq)] if i == 0 and g > 0 else []))
            pv = b.cnt("pe")
            vop(lambda e: e.tensor_copy(out=nrow[0:1, 0:512], in_=pr[0:1, 0:512]),
                waits=[("pe", pv)])
            ve_nq = b.cnt("ve")
            vev = b.cnt("ve")
            act(nrow[0:1, 0:512], nrow[0:1, 0:512], AF.Ln, waits=[("ve", vev)])
            act(nrow[0:1, 0:512], nrow[0:1, 0:512], AF.Exp, scale=0.5)
            sev = b.cnt("se")
            vop(lambda e, g=g: e.reduce_max(out=cq[0:1, 2 * g:2 * g + 1],
                                            in_=nrow[0:1, 0:256], axis=AX.X),
                waits=[("se", sev)])
            vop(lambda e, g=g: e.reduce_max(out=cq[0:1, 2 * g + 1:2 * g + 2],
                                            in_=nrow[0:1, 256:512], axis=AX.X))
            vch(lambda e: e.tensor_scalar(out=nrow[0:1, 0:512], in0=nrow[0:1, 0:512],
                                          scalar1=32.0, scalar2=1e-8,
                                          op0=ALU.mult, op1=ALU.add))
            vch(lambda e: e.reciprocal(out=nrow[0:1, 512:1024], in_=nrow[0:1, 0:512]))
            vch(lambda e: e.tensor_scalar_mul(nrow[0:1, 512:1024], nrow[0:1, 512:1024],
                                              32.0 * PI))
            wfq = [("dw", dw_fq[g - 1])] if g >= 1 else []
            vch(lambda e: e.tensor_copy(out=fqsl[0:1, 0:512], in_=nrow[0:1, 512:1024]),
                waits=wfq)
            vev = b.cnt("ve")
            b.do("scalar", lambda e, g=g: e.dma_start(out=fq_s[0:1, g * 512:(g + 1) * 512],
                                                      in_=fqsl[0:1, 0:512]),
                 inc="dw", inck=16, waits=[("ve", vev)])
            dw_fq[g] = b.cnt("dw")
        vch(lambda e: e.tensor_scalar(out=cq[0:1, 16:32], in0=cq[0:1, 0:16],
                                      scalar1=mxk[0:1, 0:1], scalar2=-1.0,
                                      op0=ALU.mult, op1=ALU.mult))
        vch(lambda e: e.tensor_copy(out=cqb_bf[0:1, 0:16], in_=cq[0:1, 16:32]))
        if DBG and bi == DBGB:
            b.wait("gpsimd", "dw", b.cnt("dw"))
            dma("gpsimd", dbg["d_qt"][:, :], qts[:, :], "dg")
            dma("gpsimd", dbg["d_fq"][:, :], fq_s[:, :], "dg")
            b.do("scalar", lambda e: e.dma_start(out=dbg["d_cq"][:, :], in_=cq[:, :]),
                 inc="dw", inck=16, waits=[("ve", b.cnt("ve"))])

    def bisect_init():
        vop(lambda e: e.memset(th, 0.5))
        vop(lambda e: e.memset(lo_, 0.0))
        vop(lambda e: e.memset(hi_, 1.001))

    def bisect_iter():
        vch(lambda e: e.tensor_scalar(
            out=bscr[:, :], in0=Sm8[:, 0:512],
            scalar1=th, scalar2=0.0, op0=ALU.is_ge, op1=ALU.add,
            accum_out=ccols[:, 0:1]))
        for c in range(1, 8):
            vop(lambda e, c=c: e.tensor_scalar(
                out=bscr[:, :], in0=Sm8[:, c * 512:(c + 1) * 512],
                scalar1=th, scalar2=0.0, op0=ALU.is_ge, op1=ALU.add,
                accum_out=ccols[:, c:c + 1]))
        vch(lambda e: e.reduce_sum(out=cntc, in_=ccols, axis=AX.X))
        vch(lambda e: e.tensor_scalar(out=mvec, in0=cntc, scalar1=512.0, scalar2=0.0,
                                      op0=ALU.is_ge, op1=ALU.add))
        vch(lambda e: e.tensor_scalar(out=mivec, in0=mvec, scalar1=-1.0, scalar2=1.0,
                                      op0=ALU.mult, op1=ALU.add))
        vch(lambda e: e.tensor_sub(dvec, th, lo_))
        vch(lambda e: e.tensor_mul(dvec, dvec, mvec))
        vch(lambda e: e.tensor_add(lo_, lo_, dvec))
        vch(lambda e: e.tensor_sub(dvec, th, hi_))
        vch(lambda e: e.tensor_mul(dvec, dvec, mivec))
        vch(lambda e: e.tensor_add(hi_, hi_, dvec))
        vch(lambda e: e.tensor_add(th, lo_, hi_))
        vch(lambda e: e.tensor_scalar_mul(th, th, 0.5))

    def phase_p2(bi):
        bisect_init()
        ve_fq = {}
        se_sw2d = {}
        ve_rsd = {}
        se_exp = {}
        ve_cp = {}
        se_abs = {}
        pe_sc = {}
        pe_sw2 = {}
        dw_at = {}
        pe_fqmm = {}

        def emit_fqcneg(qb):
            wl_ = [("pe", pe_fqmm[qb - 2])] if qb >= 2 else []
            dma("sync", fqsl[0:1, (qb % 2) * 256:(qb % 2) * 256 + 256],
                fq_s[0:1, qb * 256:(qb + 1) * 256], "dx", waits=wl_)
            dxv = b.cnt("dx")
            w_ = [("dx", dxv)]
            if qb >= 2:
                w_.append(("se", se_sw2d[qb - 2]))
            pe_fqmm[qb] = mm(pa[0][:, 256:512], ones1[0:1, :],
                             fqsl[0:1, (qb % 2) * 256:(qb % 2) * 256 + 256], True, True,
                             inc="pe", waits=w_)
            mm(pa[0][:, 0:1], ones1[0:1, :], cqb_bf[0:1, qb:qb + 1], True, True, inc="pe")
            pv = b.cnt("pe")
            vop(lambda e: e.tensor_copy(out=fqbb[:, :], in_=pa[0][:, 256:512]),
                waits=[("pe", pv)])
            w2_ = [("se", se_exp[qb - 2])] if qb >= 2 else []
            vop(lambda e, qb=qb: e.tensor_copy(out=cneg[qb % 2], in_=pa[0][:, 0:1]),
                waits=w2_)
            ve_fq[qb] = b.cnt("ve")

        def emit_scores(qb):
            w_ = [("pe", pe_sc[qb - 2])] if qb >= 2 else []
            b.do("sync", lambda e, qb=qb: e.dma_start(
                out=qsl[qb % 2][:, 0:2048],
                in_=bass.AP(qts, qb * 256, [[4096, 128], [128 * 4096, 8], [1, 256]])),
                inc="dx", inck=16, waits=w_)
            dxq = b.cnt("dx")
            for kt in range(NKT):
                bank = pb[kt % 2]
                for i in range(8):
                    w_ = []
                    if i == 0:
                        if kt < 2:
                            w_.append(("dx", dxq))
                            if qb >= 1:
                                w_.append(("ve", ve_cp[(qb - 1, 30 + kt)]))
                        else:
                            w_.append(("ve", ve_cp[(qb, kt - 2)]))
                    mm(bank[:, 0:256], Kt(i)[:, kt * 128:(kt + 1) * 128],
                       qsl[qb % 2][:, i * 256:(i + 1) * 256],
                       i == 0, i == 7, inc="pe" if i == 7 else None, waits=w_)
                pv = b.cnt("pe")
                if kt == NKT - 1:
                    pe_sc[qb] = pv
                w2_ = [("pe", pv)]
                if qb >= 1 and kt < 2:
                    w2_.append(("se", se_abs[qb - 1]))
                vop(lambda e, kt=kt, bank=bank: e.tensor_copy(out=Bbuf(kt), in_=bank[:, 0:256]),
                    waits=w2_)
                ve_cp[(qb, kt)] = b.cnt("ve")

        def emit_batches(qb):
            for kt in range(NKT):
                w_ = [("ve", ve_cp[(qb, kt)])]
                if kt == 0 and qb >= 1:
                    w_.append(("pe", pe_sw2[qb - 1]))
                act(Abuf(kt), Bbuf(kt), AF.Abs, waits=w_)
            se_abs[qb] = b.cnt("se")
            for kt in range(NKT):
                w_ = [("se", se_abs[qb]), ("ve", ve_fq[qb])] if kt == 0 else []
                vop(lambda e, kt=kt: e.tensor_mul(Abuf(kt), Abuf(kt), fqbb[:, 0:256]),
                    waits=w_)
            ve_t1 = b.cnt("ve")
            for kt in range(NKT):
                w_ = [("ve", ve_t1)] if kt == 0 else []
                act(Abuf(kt), Abuf(kt), AF.Sin, scale=ickc[:, kt:kt + 1], bias=halfpi,
                    waits=w_)
            se_sin = b.cnt("se")
            for kt in range(NKT):
                w_ = [("se", se_sin)] if kt == 0 else []
                vop(lambda e, kt=kt: e.tensor_mul(Abuf(kt), Abuf(kt), Bbuf(kt)), waits=w_)
            ve_w = b.cnt("ve")
            for kt in range(NKT):
                w_ = [("ve", ve_w)] if kt == 0 else []
                act(Abuf(kt), Abuf(kt), AF.Exp, bias=cneg[qb % 2], waits=w_)
            se_exp[qb] = b.cnt("se")

        def emit_sweeps(qb):
            for kt in range(NKT):
                w_ = []
                if kt == 0:
                    w_ = [("se", se_exp[qb])]
                    if qb >= 1:
                        w_ += [("se", se_sw2d[qb - 1]), ("ve", ve_rsd[qb - 1])]
                    if (qb + 1) in ve_fq:
                        w_.append(("ve", ve_fq[qb + 1]))
                for dc in range(4):
                    mm(pa[dc][:, 0:256], Vt(kt)[:, dc * 128:(dc + 1) * 128], Abuf(kt),
                       kt == 0, kt == NKT - 1, waits=(w_ if dc == 0 else ()))
                mm(pr[:, 0:256], vones[:, :], Abuf(kt), kt == 0, kt == NKT - 1,
                   inc="pe" if kt == NKT - 1 else None)
            pv = b.cnt("pe")
            for dc in range(4):
                w_ = [("pe", pv)] if dc == 0 else []
                if dc == 0 and qb >= 1:
                    w_.append(("dw", dw_at[qb - 1]))
                act(atile(dc), pa[dc][:, 0:256], AF.Copy, waits=w_)
            se_d1 = b.cnt("se")
            vop(lambda e: e.tensor_copy(out=atile(8)[0:1, 0:256], in_=pr[0:1, 0:256]),
                waits=[("pe", pv)])
            vev = b.cnt("ve")
            b.do("scalar", lambda e, qb=qb: e.dma_start(
                out=rs_s[0:1, qb * 256:(qb + 1) * 256], in_=atile(8)[0:1, 0:256]),
                inc="dw", inck=16, waits=[("ve", vev)])
            ve_rsd[qb] = b.cnt("ve")
            for kt in range(NKT):
                w_ = [("se", se_d1)] if kt == 0 else []
                for dc in range(4):
                    mm(pa[dc][:, 0:256], Vt(kt)[:, (4 + dc) * 128:(5 + dc) * 128], Abuf(kt),
                       kt == 0, kt == NKT - 1,
                       inc="pe" if (kt == NKT - 1 and dc == 3) else None,
                       waits=(w_ if dc == 0 else ()))
            pe_sw2[qb] = b.cnt("pe")
            pv = pe_sw2[qb]
            for dc in range(4):
                act(atile(4 + dc), pa[dc][:, 0:256], AF.Copy,
                    waits=([("pe", pv)] if dc == 0 else []))
            se_sw2d[qb] = b.cnt("se")
            for dc in range(8):
                b.do("scalar", lambda e, dc=dc, qb=qb: e.dma_start(
                    out=attnT_s[dc * 128:(dc + 1) * 128, qb * 256:(qb + 1) * 256],
                    in_=atile(dc)), inc="dw", inck=16)
            dw_at[qb] = b.cnt("dw")

        emit_fqcneg(0)
        for qb in range(NQB):
            emit_scores(qb)
            if qb >= 1:
                emit_sweeps(qb - 1)
            if qb + 1 < NQB:
                emit_fqcneg(qb + 1)
            emit_batches(qb)
            for _ in range(BIS_PER_QB):
                bisect_iter()
            for z in range(4):
                r0 = (qb * 4 + z) * 64
                b.do("scalar", lambda e, r0=r0: e.dma_start(
                    out=bass.AP(acc_s, r0 * D, [[512, 128], [1, 512]]), in_=zerot),
                    inc="dw", inck=16)
        emit_sweeps(NQB - 1)
        if DBG and bi == DBGB:
            b.wait("gpsimd", "dw", b.cnt("dw"))
            dma("gpsimd", dbg["d_at"][:, :], attnT_s[:, :], "dg")
            dma("gpsimd", dbg["d_rs"][:, :], rs_s[:, :], "dg")

    def phase_p25(bi):
        for _ in range(BIS_PER_QB * NQB, 52):
            bisect_iter()
        vch(lambda e: e.tensor_scalar(
            out=bscr[:, :], in0=Sm8[:, 0:512],
            scalar1=lo_, scalar2=0.0, op0=ALU.is_gt, op1=ALU.add,
            accum_out=ccols[:, 0:1]))
        for c in range(1, 8):
            vop(lambda e, c=c: e.tensor_scalar(
                out=bscr[:, :], in0=Sm8[:, c * 512:(c + 1) * 512],
                scalar1=lo_, scalar2=0.0, op0=ALU.is_gt, op1=ALU.add,
                accum_out=ccols[:, c:c + 1]))
        vch(lambda e: e.reduce_sum(out=ngt, in_=ccols, axis=AX.X))
        vch(lambda e: e.tensor_scalar(out=needc, in0=ngt, scalar1=-1.0, scalar2=512.0,
                                      op0=ALU.mult, op1=ALU.add))
        vch(lambda e: e.tensor_scalar(out=eqm, in0=Sm8[:, :], scalar1=lo_, scalar2=None,
                                      op0=ALU.is_equal))
        vch(lambda e: e.tensor_scalar(out=gtm, in0=Sm8[:, :], scalar1=lo_, scalar2=None,
                                      op0=ALU.is_gt))
        eqrank = AB0[:, 0:8192].bitcast(F32)[0:NE, 0:4096]
        rank8 = AB0[:, 8192:16384].bitcast(F32)[0:NE, 0:4096]
        vch(lambda e: e.tensor_tensor_scan(out=eqrank, data0=eqm, data1=eqm,
                                           initial=0.0, op0=ALU.add, op1=ALU.bypass))
        vch(lambda e: e.scalar_tensor_tensor(out=eqm, in0=eqrank, scalar=needc,
                                             in1=eqm, op0=ALU.is_le, op1=ALU.mult))
        vch(lambda e: e.tensor_add(gtm, gtm, eqm))
        vch(lambda e: e.tensor_tensor_scan(out=rank8, data0=gtm, data1=gtm,
                                           initial=0.0, op0=ALU.add, op1=ALU.bypass))
        vch(lambda e: e.tensor_mul(rank8, rank8, gtm))
        vev = b.cnt("ve")
        for grp in range(4):
            w_ = [("ve", vev if grp == 0 else b.cnt("ve"))]
            for t8 in range(8):
                t = grp * 8 + t8
                tp(pr[0:128, t8 * 8:(t8 + 1) * 8], rank8[0:8, t * 128:(t + 1) * 128],
                   eyef[:, :], inc="pe" if t8 == 7 else None,
                   waits=(w_ if t8 == 0 else ()))
            pv = b.cnt("pe")
            vop(lambda e, grp=grp: e.tensor_copy(
                out=mrankT[:, grp * 64:(grp + 1) * 64], in_=pr[0:128, 0:64]),
                waits=[("pe", pv)])
        b.barrier()
        for d_ in range(8):
            dma("sync", WoTt(d_), wobf[0:1024, d_ * 128:(d_ + 1) * 128], "dx", transpose=True)
        dxWo = b.cnt("dx")
        dma("sync", rsld, rs_s[0:1, :], "dx")
        dxrs = b.cnt("dx")
        pt2 = pt[:, 0:16].rearrange("p (a b) -> p a b", b=2)
        for grp in range(4):
            w_ = [("dx", dxrs)] if grp == 0 else [("ve", b.cnt("ve"))]
            for c8_ in range(8):
                c = grp * 8 + c8_
                tp(pt[0:128, 2 * c8_:2 * c8_ + 1], rsld[0:1, c * 128:(c + 1) * 128],
                   eyeb[:, :], inc="pe" if c8_ == 7 else None,
                   waits=(w_ if c8_ == 0 else ()))
            pv = b.cnt("pe")
            vop(lambda e, grp=grp: e.tensor_copy(out=irsc[:, grp * 8:(grp + 1) * 8],
                                                 in_=pt2[:, :, 0:1]),
                waits=[("pe", pv)])
        vch(lambda e: e.reciprocal(out=irsc[:, :], in_=irsc[:, :]))
        ve_irs = b.cnt("ve")
        b.do("gpsimd", lambda e: e.iota(jiota, pattern=[[1, 512]], base=1,
                                        channel_multiplier=0,
                                        allow_small_or_imprecise_dtypes=True), inc="ge")
        dma("gpsimd", packed, h_hilo[:, :], "dg")
        ge_i = b.cnt("ge"); dg_p = b.cnt("dg")
        dma("sync", AFa[0:1, 1024:2048], bo_d[0:1, :], "dx")
        vop(lambda e: e.tensor_copy(out=bo_row, in_=AFa[0:1, 1024:2048]),
            waits=[("dx", b.cnt("dx"))])
        smt3 = smt_bf.rearrange("p (t e) -> p t e", e=8)
        ve_ig = 0
        for ei in range(NE):
            w_ = [("dg", dg_p)]
            if ei > 0:
                w_.append(("pe", b.cnt("pe")))
            vop(lambda e, ei=ei: e.tensor_copy(out=packed3[:, :, 64:65],
                                               in_=smt3[:, :, ei:ei + 1]),
                waits=w_)
            ve_pk = b.cnt("ve")
            pe_sel = {}
            for t in range(NKT):
                w2_ = [("ge", ge_i), ("ve", ve_pk)] if t < 2 else [("pe", pe_sel[t - 2])]
                vop(lambda e, t=t, ei=ei: e.tensor_scalar(
                    out=selT[t % 2], in0=jiota, scalar1=mrankT[:, t * 8 + ei:t * 8 + ei + 1],
                    scalar2=None, op0=ALU.is_equal), waits=w2_)
                vev = b.cnt("ve")
                w3_ = [("ve", vev)]
                if t == 0 and ei > 0:
                    w3_.append(("ve", ve_ig))
                pe_sel[t] = mm(pr[0:96, 0:512], packed[:, 96 * t:96 * (t + 1)], selT[t % 2],
                               t == 0, t == NKT - 1, inc="pe", waits=w3_)
            pv = b.cnt("pe")
            vop(lambda e: e.tensor_copy(out=igs, in_=pr[0:96, 0:512]), waits=[("pe", pv)])
            ve_ig = b.cnt("ve")
            for c in range(4):
                tp(pr[0:128, c:c + 1], igs[0:1, c * 128:(c + 1) * 128], eyef[:, :],
                   waits=([("ve", ve_ig)] if c == 0 else ()))
            for c in range(4):
                tp(pr[0:128, 4 + c:5 + c], igs[32:33, c * 128:(c + 1) * 128], eyef[:, :],
                   bp=32)
            for c in range(4):
                tp(pr[0:128, 8 + c:9 + c], igs[64:65, c * 128:(c + 1) * 128], eyef[:, :],
                   bp=64, inc="pe" if c == 3 else None)
            pv = b.cnt("pe")
            vop(lambda e: e.tensor_copy(out=stats[:, 0:12], in_=pr[0:128, 0:12]),
                waits=[("pe", pv)])
            vch(lambda e, ei=ei: e.tensor_scalar(
                out=icol_all[:, ei * 4:(ei + 1) * 4], in0=stats[:, 0:4], scalar1=256.0,
                scalar2=None, op0=ALU.mult, accum_out=None) if False else
                e.scalar_tensor_tensor(out=icol_all[:, ei * 4:(ei + 1) * 4],
                                       in0=stats[:, 0:4], scalar=256.0,
                                       in1=stats[:, 4:8], op0=ALU.mult, op1=ALU.add))
            vch(lambda e, ei=ei: e.tensor_copy(out=gcol_all[:, ei * 4:(ei + 1) * 4],
                                               in_=stats[:, 8:12]))
            ve_ig = b.cnt("ve")
        pe_op = {}
        se_yat = {}
        dw_ya = {}
        for qb in range(NQB):
            s_ = qb % 2
            for dc in range(8):
                w_ = []
                if dc == 0:
                    w_ = [("dx", dxWo)]
                    if qb >= 2:
                        w_.append(("pe", pe_op[(qb - 2, 3)]))
                dma("sync", asl(dc, s_),
                    attnT_s[dc * 128:(dc + 1) * 128, qb * 256:(qb + 1) * 256], "dx", waits=w_)
            dxa = b.cnt("dx")
            for qc in range(2):
                for ih in range(2):
                    k_idx = qc * 2 + ih
                    bank = pb[k_idx % 2]
                    for dc in range(8):
                        w2_ = []
                        if dc == 0:
                            w2_ = [("dx", dxa)]
                            sev = se_yat.get((qb, k_idx - 2)) if k_idx >= 2 \
                                else se_yat.get((qb - 1, k_idx + 2))
                            if sev:
                                w2_.append(("se", sev))
                        mm(bank[:, 0:512], asl(dc, s_)[:, qc * 128:(qc + 1) * 128],
                           WoTt(dc)[:, ih * 512:(ih + 1) * 512], dc == 0, False, waits=w2_)
                    mm(bank[:, 0:512],
                       rsld[0:1, qb * 256 + qc * 128:qb * 256 + (qc + 1) * 128],
                       bo_row[0:1, ih * 512:(ih + 1) * 512], False, True, inc="pe")
                    pe_op[(qb, k_idx)] = b.cnt("pe")
                    w3_ = [("pe", pe_op[(qb, k_idx)])]
                    if qb == 0 and k_idx == 0:
                        w3_.append(("ve", ve_irs))
                    if ih == 0:
                        pdw = dw_ya.get((qb - 1, qc))
                        if pdw:
                            w3_.append(("dw", pdw))
                    act(yat[qc][:, ih * 512:(ih + 1) * 512], bank[:, 0:512], AF.Identity,
                        scale=irsc[:, qb * 2 + qc:qb * 2 + qc + 1], waits=w3_)
                    se_yat[(qb, k_idx)] = b.cnt("se")
                b.do("scalar", lambda e, qb=qb, qc=qc: e.dma_start(
                    out=yattn_s[qb * 256 + qc * 128:qb * 256 + (qc + 1) * 128, :],
                    in_=yat[qc]), inc="dw", inck=16)
                dw_ya[(qb, qc)] = b.cnt("dw")
        if DBG and bi == DBGB:
            b.wait("gpsimd", "dw", b.cnt("dw"))
            dma("gpsimd", dbg["d_ya"][:, :], yattn_s[:, :], "dg")
            b.wait("scalar", "ve", b.cnt("ve"))
            b.do("scalar", lambda e: e.dma_start(out=dbg["d_ic"][:, :],
                                                 in_=icol_all[:, :].bitcast(F32)),
                 inc="dw", inck=16)
            b.do("scalar", lambda e: e.dma_start(out=dbg["d_gc"][:, :], in_=gcol_all[:, :]),
                 inc="dw", inck=16)
            b.do("scalar", lambda e: e.dma_start(out=dbg["d_mr"][:, :], in_=mrankT[:, :]),
                 inc="dw", inck=16)

    def phase_p3(bi):
        dg_w1 = {}
        dg_w2 = {}
        dg_b2 = {}
        pe_mm1 = {}
        pe_mm2 = {}
        se_gelu = {}
        se_od = {}
        dg_scat = {}
        pe_ext = {}
        se_ext = {}
        vop(lambda e: e.memset(AB2[:, 0:16384], 0.0))
        vop(lambda e: e.memset(AB3[:, 0:8192], 0.0))
        vop(lambda e: e.memset(AFa[:, 0:2048], 0.0))
        ve_clr = b.cnt("ve")
        b.wait("scalar", "ve", ve_clr)
        b.wait("gpsimd", "ve", ve_clr)
        b.wait("tensor", "ve", ve_clr)
        for h in range(8):
            load_wtile_128x("gpsimd", W1t(h), w1_d, h * 128, 128, "dg", ncols=FF)
        dg_w1[0] = b.cnt("dg")
        for ei in range(NE):
            w_ = [("pe", pe_mm2[ei - 1])] if ei > 0 else []
            b.do("gpsimd", lambda e, ei=ei: e.dma_start(out=b2row, in_=b2_d[ei:ei + 1, :]),
                 inc="dg", inck=16, waits=w_)
            dg_b2[ei] = b.cnt("dg")
            for j in range(4):
                w_ = []
                if j >= 2:
                    w_.append(("pe", pe_ext[(ei, j - 2)]))
                elif ei > 0:
                    w_.append(("pe", pe_ext[(ei - 1, j + 2)]))
                b.do("gpsimd", lambda e, ei=ei, j=j: e.indirect_dma_start(
                    out=exbuf[j % 2], out_offset=None, in_=xn_s[0:128, :],
                    in_offset=bass.IndirectOffsetOnAxis(
                        icol_all[:, ei * 4 + j:ei * 4 + j + 1], 0),
                    bounds_check=bc_reg, oob_is_err=False),
                    inc="dg", inck=16, waits=w_)
                dgv = gfence()
                for c in range(8):
                    w2_ = []
                    if c == 0:
                        w2_ = [("dg", dgv)]
                        sev = se_ext.get((ei, j - 1)) or se_ext.get((ei - 1, 3))
                        if sev:
                            w2_.append(("se", sev))
                    pe_ext[(ei, j)] = tp(pt[:, c * 128:(c + 1) * 128],
                                         exbuf[j % 2][:, c * 128:(c + 1) * 128], eyeb[:, :],
                                         inc="pe", waits=w2_)
                pv = b.cnt("pe")
                for c in range(8):
                    b.do("scalar", lambda e, c=c, j=j: e.copy(
                        out=exTt(c)[:, j * 128:(j + 1) * 128], in_=pt[:, c * 128:(c + 1) * 128]),
                        inc="se", waits=([("pe", pv)] if c == 0 else ()))
                se_ext[(ei, j)] = b.cnt("se")
            for fc in range(32):
                bank = pb[fc % 2]
                for h in range(8):
                    w_ = []
                    if h == 0:
                        w_ = [("dg", dg_w1[ei]), ("se", se_ext[(ei, 3)])]
                        if fc >= 2:
                            w_.append(("se", se_gelu[(ei, fc - 2)]))
                        elif ei > 0:
                            w_.append(("se", se_od[(ei - 1, 7)]))
                    mm(bank[:, 0:512], W1t(h)[:, fc * 128:(fc + 1) * 128], exTt(h),
                       h == 0, h == 7, inc="pe" if h == 7 else None, waits=w_)
                pe_mm1[ei] = b.cnt("pe")
                w2_ = [("pe", pe_mm1[ei])]
                if fc == 0 and ei > 0:
                    w2_.append(("pe", pe_mm2[ei - 1]))
                act(h1T(fc), bank[:, 0:512], AF.Gelu_apprx_tanh,
                    bias=b1c[:, ei * 32 + fc:ei * 32 + fc + 1], waits=w2_)
                se_gelu[(ei, fc)] = b.cnt("se")
            for f in range(8):
                w_ = [("pe", pe_mm2[ei - 1])] if (f == 0 and ei > 0) else []
                load_wtile_128x("gpsimd", AB1[:, f * 4096:(f + 1) * 4096], w2_d,
                                ei * FF + f * 512, 512, "dg", waits=w_)
            dg_w2[ei] = b.cnt("dg")
            if ei + 1 < NE:
                for h in range(8):
                    w_ = [("pe", pe_mm1[ei])] if h == 0 else []
                    load_wtile_128x("gpsimd", W1t(h), w1_d, (ei + 1) * D + h * 128, 128,
                                    "dg", waits=w_, ncols=FF)
                dg_w1[ei + 1] = b.cnt("dg")
            for jc in range(4):
                for ih in range(2):
                    k_idx = jc * 2 + ih
                    bank = pb[k_idx % 2]
                    for fc in range(32):
                        w_ = []
                        if fc == 0:
                            w_ = [("dg", dg_w2[ei]), ("se", se_gelu[(ei, 31)]),
                                  ("dg", dg_b2[ei])]
                            if k_idx >= 2:
                                w_.append(("se", se_od[(ei, k_idx - 2)]))
                        mm(bank[:, 0:512], h1T(fc)[:, jc * 128:(jc + 1) * 128],
                           W2t(fc)[:, ih * 512:(ih + 1) * 512], fc == 0, False, waits=w_)
                    mm(bank[:, 0:512], ones1[0:1, :], b2row[0:1, ih * 512:(ih + 1) * 512],
                       False, True, inc="pe")
                    pv = b.cnt("pe")
                    if k_idx == 7:
                        pe_mm2[ei] = pv
                    w2_ = [("pe", pv)]
                    if ih == 0:
                        psc = dg_scat.get((ei, jc - 2)) if jc >= 2 else dg_scat.get((ei - 1, jc + 2))
                        if psc:
                            w2_.append(("dg", psc))
                    act(out_e[jc % 2][:, ih * 512:(ih + 1) * 512], bank[:, 0:512], AF.Identity,
                        scale=gcol_all[:, ei * 4 + jc:ei * 4 + jc + 1], waits=w2_)
                    se_od[(ei, k_idx)] = b.cnt("se")
                sev = b.cnt("se")
                w3_ = [("se", sev)]
                psc = dg_scat.get((ei, jc - 1)) or dg_scat.get((ei - 1, 3))
                if psc:
                    w3_.append(("dg", psc))
                b.do("gpsimd", lambda e, ei=ei, jc=jc: e.indirect_dma_start(
                    out=acc_s[0:128, :],
                    out_offset=bass.IndirectOffsetOnAxis(
                        icol_all[:, ei * 4 + jc:ei * 4 + jc + 1], 0),
                    in_=out_e[jc % 2][:, 0:1024], in_offset=None,
                    compute_op=ALU.add, bounds_check=bc_reg, oob_is_err=False),
                    inc="dg", inck=16, waits=w3_)
                dg_scat[(ei, jc)] = gfence()
        if DBG and bi == DBGB:
            b.wait("gpsimd", "dg", b.cnt("dg"))
            dma("gpsimd", dbg["d_acc"][:, :], acc_s[:, :], "dg")

    def phase_p4(bi):
        xoff = bi * S
        dw_y = {}
        for t in range(NKT):
            sl = t % 2
            w_ = [("dw", dw_y[t - 2])] if t >= 2 else []
            dma("sync", xa4[sl], x_in[xoff + t * 128:xoff + (t + 1) * 128, :], "dx", waits=w_)
            dma("sync", ac4[sl], acc_s[t * 128:(t + 1) * 128, :], "dx")
            dma("sync", yld[sl], yattn_s[t * 128:(t + 1) * 128, :], "dx")
            dxv = b.cnt("dx")
            vop(lambda e, sl=sl: e.tensor_copy(out=cv4[sl], in_=yld[sl]),
                waits=[("dx", dxv)])
            vop(lambda e, sl=sl: e.tensor_add(xa4[sl], xa4[sl], ac4[sl]))
            vop(lambda e, sl=sl: e.tensor_add(xa4[sl], xa4[sl], cv4[sl]))
            vev = b.cnt("ve")
            b.do("scalar", lambda e, t=t, sl=sl, xo=xoff: e.dma_start(
                out=y_out[xo + t * 128:xo + (t + 1) * 128, :], in_=xa4[sl]),
                inc="dw", inck=16, waits=[("ve", vev)])
            dw_y[t] = b.cnt("dw")

    nbatches = NB if (LVL >= 99 or DBGB == 1) else 1
    for bi in range(nbatches):
        lvl_here = 99 if bi < DBGB else LVL
        phase_p0a(bi)
        b.barrier()
        if lvl_here < 2:
            break
        phase_proj_kv(bi)
        b.barrier()
        if lvl_here < 3:
            break
        phase_p1(bi)
        b.barrier()
        if lvl_here < 4:
            break
        phase_p2(bi)
        b.barrier()
        if lvl_here < 5:
            break
        phase_p25(bi)
        b.barrier()
        if lvl_here < 6:
            break
        phase_p3(bi)
        b.barrier()
        if lvl_here < 7:
            break
        phase_p4(bi)
        b.barrier()

    b.wait("sync", "dw", b.cnt("dw"))
    b.wait("sync", "dg", b.cnt("dg"))
    b.wait("sync", "dx", b.cnt("dx"))

    with es:
        with nc.Block() as block:
            @block.tensor
            def _(e):
                for fn in b.streams["tensor"]:
                    fn(e)

            @block.vector
            def _(e):
                for fn in b.streams["vector"]:
                    fn(e)

            @block.scalar
            def _(e):
                for fn in b.streams["scalar"]:
                    fn(e)

            @block.gpsimd
            def _(e):
                for fn in b.streams["gpsimd"]:
                    fn(e)

            @block.sync
            def _(e):
                for fn in b.streams["sync"]:
                    fn(e)
    return nc


def make_in_maps(x, Wq, bq, Wk, bk, Wv, bv, Wo, bo, Wg, W1, b1, W2, b2):
    common = {
        "wq": np.ascontiguousarray(np.asarray(Wq, np.float32)),
        "wk": np.ascontiguousarray(np.asarray(Wk, np.float32)),
        "wv": np.ascontiguousarray(np.asarray(Wv, np.float32)),
        "wo": np.ascontiguousarray(np.asarray(Wo, np.float32)),
        "bq": np.ascontiguousarray(np.asarray(bq, np.float32).reshape(1, D)),
        "bk": np.ascontiguousarray(np.asarray(bk, np.float32).reshape(1, D)),
        "bv": np.ascontiguousarray(np.asarray(bv, np.float32).reshape(1, D)),
        "bo": np.ascontiguousarray(np.asarray(bo, np.float32).reshape(1, D)),
        "wg": np.ascontiguousarray(np.asarray(Wg, np.float32)),
        "w1": np.ascontiguousarray(np.asarray(W1, np.float32).reshape(NE * D, FF)),
        "b1": np.ascontiguousarray(np.asarray(b1, np.float32)),
        "w2": np.ascontiguousarray(np.asarray(W2, np.float32).reshape(NE * FF, D)),
        "b2": np.ascontiguousarray(np.asarray(b2, np.float32)),
    }
    x = np.asarray(x, np.float32)
    in_maps = []
    for c in range(8):
        m = dict(common)
        m["x"] = np.ascontiguousarray(x[c * NB:(c + 1) * NB].reshape(NB * S, D))
        in_maps.append(m)
    return in_maps


def kernel(L0_attn_x, L0_attn_Wq, L0_attn_bq, L0_attn_Wk, L0_attn_bk,
           L0_attn_Wv, L0_attn_bv, L0_attn_Wo, L0_attn_bo,
           L0_ffn_Wg, L0_ffn_W1, L0_ffn_b1, L0_ffn_W2, L0_ffn_b2):
    nc = build_program("full")
    in_maps = make_in_maps(L0_attn_x, L0_attn_Wq, L0_attn_bq, L0_attn_Wk, L0_attn_bk,
                           L0_attn_Wv, L0_attn_bv, L0_attn_Wo, L0_attn_bo,
                           L0_ffn_Wg, L0_ffn_W1, L0_ffn_b1, L0_ffn_W2, L0_ffn_b2)
    res = run_bass_kernel_spmd(nc, in_maps, core_ids=list(range(8)))
    try:
        import pickle
        with open("/tmp/kdbg.pkl", "wb") as fh:
            pickle.dump({f"c{c}_{k}": np.asarray(v) for c in range(8)
                         for k, v in res.results[c].items() if k != "y"}, fh)
    except Exception as ex:
        print("dbg save failed", ex)
    y = np.stack([np.asarray(res.results[c]["y"]).reshape(NB, S, D) for c in range(8)])
    return y.reshape(16, S, D)


# version 10
# speedup vs baseline: 1.0156x; 1.0156x over previous
import numpy as np
from contextlib import ExitStack

import concourse.bass as bass
import concourse.mybir as mybir
from concourse.bass_utils import run_bass_kernel_spmd

F32 = mybir.dt.float32
BF16 = mybir.dt.bfloat16
I32 = mybir.dt.int32
AF = mybir.ActivationFunctionType
ALU = mybir.AluOpType
AX = mybir.AxisListType
PI = float(np.pi)

NB, S, D, NE, FF = 2, 4096, 1024, 8, 4096
QB = 256
NQB = S // QB
NKT = S // 128
NG = 8
BIS_PER_QB = 3

ENGINES = ("tensor", "vector", "scalar", "gpsimd", "sync")
STAGE_LEVELS = {"p0a": 1, "p0": 2, "p1": 3, "p2": 4, "p25": 5, "p3": 6, "full": 99}


class Bld:
    def __init__(self, sems):
        self.streams = {e: [] for e in ENGINES}
        self.sems = sems
        self.n = {k: 0 for k in sems}

    def do(self, eng, fn, inc=None, inck=1, waits=()):
        if inc is not None:
            self.n[inc] += inck
        wl = [(self.sems[s], v) for (s, v) in waits if v]

        def run(e, fn=fn, wl=wl, sem=(self.sems[inc] if inc else None), inck=inck):
            for sh, v in wl:
                e.wait_ge(sh, v)
            instr = fn(e)
            if sem is not None:
                if instr is None:
                    instr = e.nop()
                instr.then_inc(sem, inck)
        self.streams[eng].append(run)
        return self.n[inc] if inc else None

    def wait(self, eng, sem, val):
        sh = self.sems[sem]
        self.streams[eng].append(lambda e: e.wait_ge(sh, val))

    def cnt(self, sem):
        return self.n[sem]

    def barrier(self):
        for s in ("dx", "dw", "dg"):
            self.wait("sync", s, self.n[s])
        for e in ENGINES:
            self.do(e, lambda en: en.nop(), inc="bar")
        v = self.n["bar"]
        for e in ENGINES:
            self.wait(e, "bar", v)


def build_program(stage="full"):
    import os
    DBGB = int(os.environ.get("KDBG_BATCH", "0"))
    if stage.startswith("b1"):
        DBGB = 1
        stage = stage[2:]
    LVL = STAGE_LEVELS[stage]
    DBG = True

    nc = bass.Bass("TRN2", target_bir_lowering=False)

    x_in = nc.dram_tensor("x", [NB * S, D], F32, kind="ExternalInput")
    wq = nc.dram_tensor("wq", [D, D], F32, kind="ExternalInput")
    wk = nc.dram_tensor("wk", [D, D], F32, kind="ExternalInput")
    wv = nc.dram_tensor("wv", [D, D], F32, kind="ExternalInput")
    wo = nc.dram_tensor("wo", [D, D], F32, kind="ExternalInput")
    bq_d = nc.dram_tensor("bq", [1, D], F32, kind="ExternalInput")
    bk_d = nc.dram_tensor("bk", [1, D], F32, kind="ExternalInput")
    bv_d = nc.dram_tensor("bv", [1, D], F32, kind="ExternalInput")
    bo_d = nc.dram_tensor("bo", [1, D], F32, kind="ExternalInput")
    wg_d = nc.dram_tensor("wg", [NE, D], F32, kind="ExternalInput")
    w1_d = nc.dram_tensor("w1", [NE * D, FF], F32, kind="ExternalInput")
    b1_d = nc.dram_tensor("b1", [NE, FF], F32, kind="ExternalInput")
    w2_d = nc.dram_tensor("w2", [NE * FF, D], F32, kind="ExternalInput")
    b2_d = nc.dram_tensor("b2", [NE, D], F32, kind="ExternalInput")
    y_out = nc.dram_tensor("y", [NB * S, D], F32, kind="ExternalOutput")

    xn_s = nc.dram_tensor("xn_s", [S, D], BF16)
    qts = nc.dram_tensor("qts", [D, S], BF16)
    attnT_s = nc.dram_tensor("attnT_s", [D, S], BF16)
    yattn_s = nc.dram_tensor("yattn_s", [S, D], BF16)
    acc_s = nc.dram_tensor("acc_s", [S, D], F32)
    fq_s = nc.dram_tensor("fq_s", [1, S], BF16)
    rs_s = nc.dram_tensor("rs_s", [1, S], BF16)
    fence_s = nc.dram_tensor("fence_s", [128, 4], F32)
    wqbf = nc.dram_tensor("wqbf", [D, D], BF16)
    wkbf = nc.dram_tensor("wkbf", [D, D], BF16)
    wvbf = nc.dram_tensor("wvbf", [D, D], BF16)
    wobf = nc.dram_tensor("wobf", [D, D], BF16)

    dbg = {}
    if DBG:
        def dbgt(nm, sh, dt):
            dbg[nm] = nc.dram_tensor(nm, sh, dt, kind="ExternalOutput")
        dbgt("d_xn", [S, D], BF16); dbgt("d_sm", [NE, S], BF16)
        dbgt("d_kt", [D, S], BF16); dbgt("d_v", [S, D], BF16)
        dbgt("d_ick", [128, 32], F32)
        dbgt("d_qt", [D, S], BF16); dbgt("d_fq", [1, S], BF16)
        dbgt("d_cq", [1, 64], F32)
        dbgt("d_at", [D, S], BF16); dbgt("d_rs", [1, S], BF16)
        dbgt("d_ya", [S, D], BF16); dbgt("d_ic", [128, 32], F32)
        dbgt("d_gc", [128, 32], F32); dbgt("d_mr", [128, 256], F32)
        dbgt("d_acc", [S, D], F32)
        dbgt("d_c8", [128, 8], F32)
        dbgt("d_st", [128, 12], F32)
        dbgt("d_sta", [128, 384], F32)
        dbgt("d_xs", [S, D], F32)

    eye_np = np.eye(128, dtype=np.float32)
    vones_np = np.zeros((128, 128), dtype=np.float32); vones_np[:, 0] = 1.0
    ones1_np = np.ones((1, 128), dtype=np.float32)
    hilo_np = np.zeros((128, 96 * NKT), dtype=np.float32)
    for t in range(NKT):
        toks = t * 128 + np.arange(128)
        hilo_np[:, 96 * t] = toks // 256
        hilo_np[:, 96 * t + 32] = toks % 256
    h_eye = nc.inline_tensor(eye_np, "c_eye")
    h_vones = nc.inline_tensor(vones_np, "c_vones")
    h_ones1 = nc.inline_tensor(ones1_np, "c_ones1")
    h_hilo = nc.inline_tensor(hilo_np, "c_hilo")

    es = ExitStack()
    sbt = lambda nm, sh, dt: es.enter_context(nc.sbuf_tensor(nm, sh, dt))
    pst = lambda nm, sh, dt: es.enter_context(nc.psum_tensor(nm, sh, dt))

    AB0 = sbt("AB0", [128, 32768], BF16)
    AB1 = sbt("AB1", [128, 32768], BF16)
    AB2 = sbt("AB2", [128, 20480], BF16)
    AB3 = sbt("AB3", [128, 8192], BF16)
    AFa = sbt("AFa", [128, 3072], F32)
    AT = sbt("AT", [128, 2304], BF16)

    eyef = sbt("eyef", [128, 128], F32)
    eyeb = sbt("eyeb", [128, 128], BF16)
    vones = sbt("vones", [128, 128], BF16)
    ones1 = sbt("ones1", [1, 128], BF16)
    onesc = sbt("onesc", [128, 1], BF16)
    fqsl = sbt("fqsl", [1, 512], BF16)
    fqbb = sbt("fqbb", [128, 256], BF16)
    smt_bf = sbt("smt_bf", [128, 256], BF16)
    b1c = sbt("b1c", [128, 256], F32)
    wgt = sbt("wgt", [128, 64], BF16)
    bqc = sbt("bqc", [128, 8], F32)
    bkc = sbt("bkc", [128, 8], F32)
    ickc = sbt("ickc", [128, 32], F32)
    irsc = sbt("irsc", [128, 32], F32)
    icol_all = sbt("icol_all", [128, 32], I32)
    gcol_all = sbt("gcol_all", [128, 32], F32)
    bis = sbt("bis", [8, 32], F32)
    bscr = sbt("bscr", [8, 512], BF16)
    c8 = sbt("c8", [128, 8], F32)
    cq = sbt("cq", [1, 64], F32)
    cqb_bf = sbt("cqb_bf", [1, 32], BF16)
    probst = sbt("probst", [128, 8], F32)
    stats = sbt("stats", [128, 12], F32)

    mxk = sbt("mxk", [1, 8], F32)

    pb = [pst("pb0", [128, 512], F32), pst("pb1", [128, 512], F32)]
    pa = [pst(f"pa{i}", [128, 512], F32) for i in range(4)]
    pr = pst("pr", [128, 512], F32)
    pt = pst("pt", [128, 1024], BF16)

    sems = {}
    for s_name in ("dx", "dw", "dg", "pe", "ve", "se", "ge", "bar"):
        sems[s_name] = es.enter_context(nc.semaphore(s_name))
    b = Bld(sems)
    bc_reg = nc.gpsimd.alloc_register("bc4095")
    b.do("gpsimd", lambda e: e.reg_mov(bc_reg, S - 1) or e.nop())

    Kt = lambda i: AB0[:, i * 4096:(i + 1) * 4096]
    Vt = lambda t: AB1[:, t * 1024:(t + 1) * 1024]
    W1t = lambda h: AB0[:, h * 4096:(h + 1) * 4096]
    W2t = lambda f: AB1[:, f * 1024:(f + 1) * 1024]
    WoTt = lambda d_: AB1[:, d_ * 1024:(d_ + 1) * 1024]
    xnT_g = lambda h: AB2[:, h * 512:(h + 1) * 512]
    sqb = [AB2[:, 4096:4608], AB2[:, 4608:5120]]
    xn_slot = [AB2[:, 5120:6144], AB2[:, 6144:7168]]
    qtile = [AB2[:, 7168:7680], AB2[:, 7680:8192]]
    sqd = AB2[:, 8192:9216]
    wproj = lambda h: AB2[:, 10240 + h * 1024:10240 + (h + 1) * 1024]
    Abuf = lambda kt: AB2[:, kt * 256:(kt + 1) * 256]
    Bbuf = lambda kt: AB2[:, 8192 + kt * 256:8192 + (kt + 1) * 256]
    qsl = [AB2[:, 16384:18432], AB2[:, 18432:20480]]
    eqm = AB2[0:NE, 0:4096]
    gtm = AB2[0:NE, 4096:8192]
    ab2f = AB2[:, 0:4096].bitcast(F32)
    jiota = ab2f[:, 0:512]
    igs = ab2f[0:96, 512:1024]
    selT = [AB2[:, 4096:4608], AB2[:, 4608:5120]]
    packed = AB2[:, 5120:8192]
    packed3 = packed.rearrange("p (t c) -> p t c", c=96)
    rsld = AB2[0:1, 8192:12288]
    asl = lambda d_, s_: AB2[:, 12288 + (s_ * 8 + d_) * 256:12288 + (s_ * 8 + d_ + 1) * 256]
    yat = [AB2[:, 16384:17408], AB2[:, 17408:18432]]
    h1T = lambda f: AB2[:, f * 512:(f + 1) * 512]
    exTt = lambda h: AB3[:, h * 512:(h + 1) * 512]
    exbuf = [AB3[:, 4096:5120], AB3[:, 5120:6144]]
    b2row = AB3[0:1, 6144:7168]
    Sm8 = AB3[0:NE, 0:4096]
    yld = [AB2[:, 0:1024], AB2[:, 1024:2048]]
    p4f = AB2[:, 2048:8192].bitcast(F32)
    xa4 = [AFa[:, 0:1024], p4f[:, 0:1024]]
    ac4 = [AFa[:, 1024:2048], p4f[:, 1024:2048]]
    cv4 = [AFa[:, 2048:3072], p4f[:, 2048:3072]]
    xstage = [AFa[:, 0:1024], AFa[:, 1024:2048]]
    zerot = AFa[:, 2048:2560]
    nrow = AFa[0:1, 0:1024]
    out_e = [AFa[:, 0:1024], AFa[:, 1024:2048]]
    i_row = AFa[0:1, 0:512]
    g_row = AFa[0:1, 512:1024]
    prep_row = AFa[0:32, 0:128]
    atile = lambda d_: AT[:, d_ * 256:(d_ + 1) * 256]
    bv_row = AT[0:1, 0:1024]
    bo_row = AT[0:1, 1024:2048]
    mrankT = AT[:, 0:512].bitcast(F32)
    th = bis[:, 0:1]; lo_ = bis[:, 1:2]; hi_ = bis[:, 2:3]
    cntc = bis[:, 3:4]; mvec = bis[:, 4:5]; mivec = bis[:, 5:6]
    dvec = bis[:, 6:7]; needc = bis[:, 7:8]; ngt = bis[:, 16:17]
    ccols = bis[0:8, 8:16]
    halfpi = c8[:, 0:1]; epsc = c8[:, 1:2]
    cneg = [c8[:, 2:3], c8[:, 3:4]]
    inv32c = c8[:, 4:5]
    s_sum = stats[:, 0:1]; s_sq = stats[:, 1:2]
    s_mean = stats[:, 3:4]; s_msq = stats[:, 4:5]; s_var = stats[:, 5:6]
    s_rstd = stats[:, 6:7]; s_nmr = stats[:, 7:8]; s_rcp = stats[:, 8:9]

    def mm(ap, lhsT, rhs, start, stop, inc=None, waits=()):
        return b.do("tensor",
                    lambda e, a=ap, l=lhsT, r=rhs, s1=start, s2=stop:
                    e.matmul(a, l, r, start=s1, stop=s2), inc=inc, waits=waits)

    def tp(ap, in_ap, ident, inc=None, waits=(), bp=0):
        np_ = in_ap.shape[0]
        ident = ident[bp:bp + np_, bp:bp + np_]
        return b.do("tensor",
                    lambda e, a=ap, i=in_ap, d=ident: e.transpose(a, i, d),
                    inc=inc, waits=waits)

    def act(out_ap, in_ap, func, scale=1.0, bias=0.0, accum=None, inc="se", waits=()):
        return b.do("scalar",
                    lambda e, o=out_ap, i=in_ap, f=func, s=scale, bi=bias, a=accum:
                    e.activation(out=o, in_=i, func=f, scale=s, bias=bi, accum_out=a),
                    inc=inc, waits=waits)

    def vop(fn, inc="ve", waits=()):
        return b.do("vector", fn, inc=inc, waits=waits)

    def vch(fn, waits=()):
        return b.do("vector", fn, inc="ve", waits=[("ve", b.cnt("ve"))] + list(waits))

    def dma(eng, out_ap, in_ap, inc, waits=(), transpose=False):
        if transpose:
            f = lambda e, o=out_ap, i=in_ap: e.dma_start(out=o, in_=i, transpose=True)
        else:
            f = lambda e, o=out_ap, i=in_ap: e.dma_start(out=o, in_=i)
        return b.do(eng, f, inc=inc, inck=16, waits=waits)

    def gfence(waits=()):
        return b.do("gpsimd", lambda e: e.dma_start(out=fence_s[:, :], in_=eyef[:, 0:4]),
                    inc="dg", inck=16, waits=waits)

    def load_wtile_128x(eng, out_ap, dram, row0, nrows, inc, waits=(), ncols=D):
        ntile = nrows // 128
        ap = bass.AP(dram, row0 * ncols, [[ncols, 128], [128 * ncols, ntile], [1, ncols]])
        return dma(eng, out_ap, ap, inc, waits=waits)

    dma("sync", eyef[:, :], h_eye[:, :], "dx")
    dma("gpsimd", eyeb[:, :], h_eye[:, :], "dg")
    dma("gpsimd", vones[:, :], h_vones[:, :], "dg")
    dma("gpsimd", ones1[:, :], h_ones1[:, :], "dg")
    dg_consts = b.cnt("dg")
    vop(lambda e: e.memset(halfpi, PI / 2))
    vop(lambda e: e.memset(epsc, 1e-5))
    vop(lambda e: e.memset(inv32c, 1.0 / 32))
    vop(lambda e: e.memset(onesc[:, :], 1.0))
    vop(lambda e: e.memset(zerot, 0.0))

    cast_jobs = [(wq, wqbf, i) for i in range(8)] + [(wk, wkbf, i) for i in range(8)] + \
                [(wv, wvbf, i) for i in range(8)] + [(wo, wobf, i) for i in range(8)]
    for idx, (w_i, w_b, i) in enumerate(cast_jobs):
        slot = AB1[:, (idx % 8) * 1024:(idx % 8 + 1) * 1024]
        waits = [("dw", 16 * (idx - 7))] if idx >= 8 else []
        b.do("gpsimd",
             lambda e, sl=slot, w=w_i, ii=i: e.dma_start(out=sl, in_=w[ii * 128:(ii + 1) * 128, :]),
             inc="dg", inck=16, waits=waits)
        gv = b.cnt("dg")
        b.do("scalar",
             lambda e, sl=slot, w=w_b, ii=i: e.dma_start(out=w[ii * 128:(ii + 1) * 128, :], in_=sl),
             inc="dw", inck=16, waits=[("dg", gv)])

    dma("sync", AFa[0:8, 0:1024], wg_d[:, :], "dx")
    dxv = b.cnt("dx")
    vop(lambda e: e.tensor_copy(out=AB3[0:8, 0:1024], in_=AFa[0:8, 0:1024]),
        waits=[("dx", dxv), ("dg", dg_consts)])
    vv = b.cnt("ve")
    for h in range(8):
        tp(pt[0:128, h * 8:(h + 1) * 8], AB3[0:8, h * 128:(h + 1) * 128], eyeb[:, :],
           inc="pe" if h == 7 else None, waits=([("ve", vv)] if h == 0 else ()))
    pv = b.cnt("pe")
    b.do("scalar", lambda e: e.copy(out=wgt[:, :], in_=pt[0:128, 0:64]), inc="se",
         waits=[("pe", pv)])

    for (bd, bc, fac) in ((bq_d, bqc, 1.0 / 32), (bk_d, bkc, 1.0)):
        dma("sync", AFa[0:8, 0:128], bd[0:1, :].rearrange("a (c d) -> (a c) d", d=128), "dx",
            waits=[("ve", b.cnt("ve")), ("se", b.cnt("se"))])
        dxv = b.cnt("dx")
        tp(pr[0:128, 0:8], AFa[0:8, 0:128], eyef[:, :], inc="pe", waits=[("dx", dxv)])
        pv = b.cnt("pe")
        vop(lambda e, bc=bc: e.tensor_copy(out=bc[:, :], in_=pr[0:128, 0:8]),
            waits=[("pe", pv)])
        if fac != 1.0:
            vch(lambda e, bc=bc, f=fac: e.tensor_scalar_mul(bc[:, :], bc[:, :], f))
        b.wait("tensor", "ve", b.cnt("ve"))
    for ei in range(NE):
        dma("sync", prep_row, b1_d[ei:ei + 1, :].rearrange("a (c d) -> (a c) d", d=128), "dx",
            waits=[("ve", b.cnt("ve"))])
        dxv = b.cnt("dx")
        tp(pr[0:128, 8:40], prep_row, eyef[:, :], inc="pe", waits=[("dx", dxv)])
        pv = b.cnt("pe")
        vop(lambda e, ei=ei: e.tensor_copy(out=b1c[:, ei * 32:(ei + 1) * 32], in_=pr[0:128, 8:40]),
            waits=[("pe", pv)])
        b.wait("tensor", "ve", b.cnt("ve"))
    dma("sync", AFa[0:1, 0:1024], bv_d[0:1, :], "dx", waits=[("ve", b.cnt("ve"))])
    vop(lambda e: e.tensor_copy(out=bv_row, in_=AFa[0:1, 0:1024]), waits=[("dx", b.cnt("dx"))])

    b.barrier()

    def phase_p0a(bi):
        xoff = bi * S
        vop(lambda e: e.memset(mxk[0:1, 0:8], 0.0))
        vop(lambda e: e.memset(zerot, 0.0))
        se_xn = {}
        dw_xn = {}
        logit_v = 0
        logit_s = 0
        ve_smdrain = 0
        pe_xnT = 0
        for g in range(NG):
            for tc in range(4):
                t_abs = g * 4 + tc
                sl = t_abs % 2
                waits = [("se", se_xn[t_abs - 2])] if t_abs >= 2 else []
                dma("sync", xstage[sl], x_in[xoff + t_abs * 128:xoff + (t_abs + 1) * 128, :],
                    "dx", waits=waits)
                dxv = b.cnt("dx")
                vop(lambda e, sl=sl: e.reduce_sum(out=s_sum, in_=xstage[sl], axis=AX.X),
                    waits=[("dx", dxv)])
                act(sqd, xstage[sl], AF.Square, accum=s_sq, waits=[("dx", dxv)])
                sev = b.cnt("se")
                vch(lambda e: e.tensor_scalar_mul(s_mean, s_sum, 1.0 / D))
                vch(lambda e: e.tensor_mul(s_msq, s_mean, s_mean))
                vch(lambda e: e.scalar_tensor_tensor(
                    out=s_var, in0=s_sq, scalar=1.0 / D, in1=s_msq,
                    op0=ALU.mult, op1=ALU.subtract), waits=[("se", sev)])
                vev = b.cnt("ve")
                act(s_var, s_var, AF.Ln, bias=epsc, waits=[("ve", vev)])
                act(s_rstd, s_var, AF.Exp, scale=-0.5)
                sev = b.cnt("se")
                vch(lambda e: e.scalar_tensor_tensor(
                    out=s_nmr, in0=s_mean, scalar=-1.0, in1=s_rstd,
                    op0=ALU.mult, op1=ALU.mult), waits=[("se", sev)])
                vev = b.cnt("ve")
                wxn = [("ve", vev)]
                if t_abs >= 2:
                    wxn.append(("dw", dw_xn[t_abs - 2]))
                act(xn_slot[sl], xstage[sl], AF.Identity, scale=s_rstd, bias=s_nmr,
                    waits=wxn)
                se_xn[t_abs] = b.cnt("se")
                if DBG and bi == DBGB:
                    sev2 = b.cnt("se")
                    b.do("scalar", lambda e, t=t_abs, sl=sl: e.dma_start(
                        out=dbg["d_xs"][t * 128:(t + 1) * 128, :], in_=xstage[sl]),
                        inc="dw", inck=16)
                    b.do("scalar", lambda e, t=t_abs: e.dma_start(
                        out=dbg["d_sta"][:, t * 12:(t + 1) * 12], in_=stats[:, :]),
                        inc="dw", inck=16)
                b.do("scalar",
                     lambda e, sl=sl, t=t_abs, xo=0: e.dma_start(
                         out=xn_s[t * 128:(t + 1) * 128, :], in_=xn_slot[sl]),
                     inc="dw", inck=16)
                dw_xn[t_abs] = b.cnt("dw")
            for h in range(8):
                w_ = [("dw", dw_xn[g * 4 + 3]), ("pe", pe_xnT)] if h == 0 else ()
                dma("sync", xnT_g(h), xn_s[g * 512:(g + 1) * 512, h * 128:(h + 1) * 128],
                    "dx", waits=w_, transpose=True)
            dxT = b.cnt("dx")
            for tc in range(4):
                t_abs = g * 4 + tc
                for h in range(8):
                    w_ = []
                    if h == 0:
                        w_ = [("dx", dxT), ("ve", logit_v), ("se", logit_s)]
                    mm(pa[0][:, 0:8], xnT_g(h)[:, tc * 128:(tc + 1) * 128],
                       wgt[:, h * 8:(h + 1) * 8], h == 0, h == 7,
                       inc="pe" if h == 7 else None, waits=w_)
                pv = b.cnt("pe")
                vop(lambda e: e.reduce_max(out=s_sum, in_=pa[0][:, 0:8], axis=AX.X, negate=True),
                    waits=[("pe", pv)])
                vev = b.cnt("ve")
                act(probst[:, :], pa[0][:, 0:8], AF.Exp, bias=s_sum, waits=[("ve", vev)])
                logit_s = b.cnt("se")
                vop(lambda e: e.reduce_sum(out=s_msq, in_=probst[:, :], axis=AX.X),
                    waits=[("se", logit_s)])
                vch(lambda e: e.reciprocal(out=s_rcp, in_=s_msq))
                vch(lambda e, t=t_abs: e.tensor_scalar(
                    out=smt_bf[:, t * 8:(t + 1) * 8], in0=probst[:, :],
                    scalar1=s_rcp, scalar2=None, op0=ALU.mult))
                logit_v = b.cnt("ve")
                tp(pt[0:8, tc * 128:(tc + 1) * 128], smt_bf[:, t_abs * 8:(t_abs + 1) * 8],
                   eyeb[:, :], inc="pe",
                   waits=[("ve", logit_v)] + ([("ve", ve_smdrain)] if tc == 0 else []))
            pv = b.cnt("pe")
            pe_xnT = pv
            vop(lambda e, g=g: e.tensor_copy(out=Sm8[:, g * 512:(g + 1) * 512],
                                             in_=pt[0:8, 0:512]),
                waits=[("pe", pv)])
            ve_smdrain = b.cnt("ve")
        if DBG and bi == DBGB:
            dma("gpsimd", dbg["d_xn"][:, :], xn_s[:, :], "dg", waits=[("dw", dw_xn[31])])
            b.do("scalar", lambda e: e.dma_start(out=dbg["d_sm"][:, :], in_=Sm8[:, :]),
                 inc="dw", inck=16, waits=[("ve", b.cnt("ve"))])
            b.do("scalar", lambda e: e.dma_start(out=dbg["d_c8"][:, :], in_=c8[:, :]),
                 inc="dw", inck=16)
            b.do("scalar", lambda e: e.dma_start(out=dbg["d_st"][:, :], in_=stats[:, :]),
                 inc="dw", inck=16)


    def phase_proj_kv(bi):
        for h in range(8):
            dma("sync", wproj(h), wkbf[0:1024, h * 128:(h + 1) * 128], "dx", transpose=True)
        dxW = b.cnt("dx")
        ve_ck = 0
        pe_ck = 0
        pe_xnT = 0
        se_kd = {}
        for g in range(NG):
            for h in range(8):
                w_ = [("pe", pe_xnT)] if h == 0 else ()
                dma("sync", xnT_g(h), xn_s[g * 512:(g + 1) * 512, h * 128:(h + 1) * 128],
                    "dx", transpose=True, waits=w_)
            dxT = b.cnt("dx")
            for i in range(8):
                bank = pb[i % 2]
                for h in range(8):
                    w_ = []
                    if h == 0:
                        w_ = [("dx", max(dxT, dxW))]
                        prev = se_kd.get((g, i - 2)) if i >= 2 else se_kd.get((g - 1, i + 6))
                        if prev:
                            w_.append(("se", prev))
                    mm(bank[:, 0:512], wproj(h)[:, i * 128:(i + 1) * 128], xnT_g(h),
                       h == 0, h == 7, inc="pe" if h == 7 else None, waits=w_)
                pv = b.cnt("pe")
                if i == 7:
                    pe_xnT = pv
                act(Kt(i)[:, g * 512:(g + 1) * 512], bank[:, 0:512], AF.Identity,
                    bias=bkc[:, i:i + 1], waits=[("pe", pv)])
                act(sqb[i % 2], Kt(i)[:, g * 512:(g + 1) * 512], AF.Square,
                    waits=([("pe", pe_ck)] if (i >= 2 or g > 0) and pe_ck else []))
                se_kd[(g, i)] = b.cnt("se")
                sev = b.cnt("se")
                pe_ck = mm(pr[0:1, 0:512], onesc[:, :], sqb[i % 2], i == 0, i == 7,
                           inc="pe",
                           waits=[("se", sev)] + ([("ve", ve_ck)] if i == 0 and g > 0 else []))
            pv = b.cnt("pe")
            vop(lambda e: e.tensor_copy(out=nrow[0:1, 0:512], in_=pr[0:1, 0:512]),
                waits=[("pe", pv)])
            vev = b.cnt("ve")
            act(nrow[0:1, 0:512], nrow[0:1, 0:512], AF.Ln, waits=[("ve", vev)])
            act(nrow[0:1, 0:512], nrow[0:1, 0:512], AF.Exp, scale=0.5)
            sev = b.cnt("se")
            vop(lambda e: e.tensor_scalar_add(nrow[0:1, 0:512], nrow[0:1, 0:512], 1e-8),
                waits=[("se", sev)])
            vch(lambda e: e.reduce_max(out=mxk[0:1, 1:2], in_=nrow[0:1, 0:512], axis=AX.X))
            vch(lambda e: e.tensor_max(mxk[0:1, 0:1], mxk[0:1, 0:1], mxk[0:1, 1:2]))
            vev = b.cnt("ve")
            for c in range(4):
                tp(pr[0:128, c:c + 1], nrow[0:1, c * 128:(c + 1) * 128], eyef[:, :],
                   inc="pe" if c == 3 else None, waits=([("ve", vev)] if c == 0 else ()))
            pv = b.cnt("pe")
            vop(lambda e, g=g: e.tensor_copy(out=ickc[:, g * 4:(g + 1) * 4], in_=pr[0:128, 0:4]),
                waits=[("pe", pv)])
            ve_ck = b.cnt("ve")
        vch(lambda e: e.reciprocal(out=gcol_all[:, :], in_=ickc[:, :]))
        vch(lambda e: e.tensor_scalar_mul(ickc[:, :], gcol_all[:, :], -1.0))
        b.barrier()
        for h in range(8):
            dma("sync", wproj(h), wvbf[0:1024, h * 128:(h + 1) * 128], "dx", transpose=True)
        dma("sync", AFa[0:1, 1024:2048], bv_d[0:1, :], "dx")
        dxbv = b.cnt("dx")
        vop(lambda e: e.tensor_copy(out=bv_row, in_=AFa[0:1, 1024:2048]),
            waits=[("dx", dxbv)])
        ve_bv = b.cnt("ve")
        b.wait("tensor", "ve", ve_bv)
        dxW = b.cnt("dx")
        se_vd = {}
        pe_xnT = 0
        for g in range(NG):
            for h in range(8):
                w_ = [("pe", pe_xnT)] if h == 0 else ()
                dma("sync", xnT_g(h), xn_s[g * 512:(g + 1) * 512, h * 128:(h + 1) * 128],
                    "dx", transpose=True, waits=w_)
            dxT = b.cnt("dx")
            for tc in range(4):
                for hf in range(2):
                    k_idx = tc * 2 + hf
                    bank = pb[k_idx % 2]
                    for h in range(8):
                        w_ = []
                        if h == 0:
                            w_ = [("dx", max(dxT, dxW))]
                            prev = se_vd.get((g, k_idx - 2)) if k_idx >= 2 else se_vd.get((g - 1, k_idx + 6))
                            if prev:
                                w_.append(("se", prev))
                        mm(bank[:, 0:512], xnT_g(h)[:, tc * 128:(tc + 1) * 128],
                           wproj(h)[:, hf * 512:(hf + 1) * 512], h == 0, False, waits=w_)
                    mm(bank[:, 0:512], ones1[0:1, :], bv_row[0:1, hf * 512:(hf + 1) * 512],
                       False, True, inc="pe")
                    pv = b.cnt("pe")
                    if k_idx == 7:
                        pe_xnT = pv
                    act(Vt(g * 4 + tc)[:, hf * 512:(hf + 1) * 512], bank[:, 0:512],
                        AF.Copy, waits=[("pe", pv)])
                    se_vd[(g, k_idx)] = b.cnt("se")
        if DBG and bi == DBGB:
            b.wait("scalar", "se", b.cnt("se"))
            for i in range(8):
                b.do("scalar", lambda e, i=i: e.dma_start(
                    out=dbg["d_kt"][i * 128:(i + 1) * 128, :], in_=Kt(i)), inc="dw", inck=16)
            for t in range(32):
                b.do("scalar", lambda e, t=t: e.dma_start(
                    out=dbg["d_v"][t * 128:(t + 1) * 128, :], in_=Vt(t)), inc="dw", inck=16)
            b.do("scalar", lambda e: e.dma_start(out=dbg["d_ick"][:, :], in_=ickc[:, :]),
                 inc="dw", inck=16, waits=[("ve", b.cnt("ve"))])

    def phase_p1(bi):
        for h in range(8):
            dma("sync", wproj(h), wqbf[0:1024, h * 128:(h + 1) * 128], "dx", transpose=True)
        dxW = b.cnt("dx")
        dw_qt = {}
        dw_fq = {}
        sq_se = {}
        pe_nq = 0
        ve_nq = 0
        pe_xnT = 0
        for g in range(NG):
            for h in range(8):
                w_ = [("pe", pe_xnT)] if h == 0 else ()
                dma("sync", xnT_g(h), xn_s[g * 512:(g + 1) * 512, h * 128:(h + 1) * 128],
                    "dx", transpose=True, waits=w_)
            dxT = b.cnt("dx")
            for i in range(8):
                bank = pb[i % 2]
                for h in range(8):
                    w_ = []
                    if h == 0:
                        w_ = [("dx", max(dxT, dxW))]
                        pdw = dw_qt.get((g, i - 2)) if i >= 2 else dw_qt.get((g - 1, i + 6))
                        if pdw:
                            w_.append(("dw", pdw))
                    mm(bank[:, 0:512], wproj(h)[:, i * 128:(i + 1) * 128], xnT_g(h),
                       h == 0, h == 7, inc="pe" if h == 7 else None, waits=w_)
                pv = b.cnt("pe")
                if i == 7:
                    pe_xnT = pv
                wq_ = [("pe", pv)]
                if sq_se.get(i - 2 if i >= 2 else (None if g == 0 else i + 6)):
                    pass
                prev_sq = sq_se.get((g, i - 2)) if i >= 2 else sq_se.get((g - 1, i + 6))
                if prev_sq:
                    wq_.append(("se", prev_sq))
                vop(lambda e, i=i, bank=bank, sl=i % 2: e.tensor_scalar(
                    out=qtile[sl], in0=bank[:, 0:512], scalar1=inv32c,
                    scalar2=bqc[:, i:i + 1], op0=ALU.mult, op1=ALU.add),
                    waits=wq_)
                vev_q = b.cnt("ve")
                b.do("scalar",
                     lambda e, i=i, g=g, sl=i % 2: e.dma_start(
                         out=qts[i * 128:(i + 1) * 128, g * 512:(g + 1) * 512], in_=qtile[sl]),
                     inc="dw", inck=16, waits=[("ve", vev_q)])
                dw_qt[(g, i)] = b.cnt("dw")
                act(sqb[i % 2], qtile[i % 2], AF.Square,
                    waits=[("ve", vev_q)] + ([("pe", pe_nq)] if (i >= 2 or g > 0) and pe_nq else []))
                sq_se[(g, i)] = b.cnt("se")
                sev = b.cnt("se")
                pe_nq = mm(pr[0:1, 0:512], onesc[:, :], sqb[i % 2], i == 0, i == 7,
                           inc="pe",
                           waits=[("se", sev)] + ([("ve", ve_nq)] if i == 0 and g > 0 else []))
            pv = b.cnt("pe")
            vop(lambda e: e.tensor_copy(out=nrow[0:1, 0:512], in_=pr[0:1, 0:512]),
                waits=[("pe", pv)])
            ve_nq = b.cnt("ve")
            vev = b.cnt("ve")
            act(nrow[0:1, 0:512], nrow[0:1, 0:512], AF.Ln, waits=[("ve", vev)])
            act(nrow[0:1, 0:512], nrow[0:1, 0:512], AF.Exp, scale=0.5)
            sev = b.cnt("se")
            vop(lambda e, g=g: e.reduce_max(out=cq[0:1, 2 * g:2 * g + 1],
                                            in_=nrow[0:1, 0:256], axis=AX.X),
                waits=[("se", sev)])
            vop(lambda e, g=g: e.reduce_max(out=cq[0:1, 2 * g + 1:2 * g + 2],
                                            in_=nrow[0:1, 256:512], axis=AX.X))
            vch(lambda e: e.tensor_scalar(out=nrow[0:1, 0:512], in0=nrow[0:1, 0:512],
                                          scalar1=32.0, scalar2=1e-8,
                                          op0=ALU.mult, op1=ALU.add))
            vch(lambda e: e.reciprocal(out=nrow[0:1, 512:1024], in_=nrow[0:1, 0:512]))
            vch(lambda e: e.tensor_scalar_mul(nrow[0:1, 512:1024], nrow[0:1, 512:1024],
                                              32.0 * PI))
            wfq = [("dw", dw_fq[g - 1])] if g >= 1 else []
            vch(lambda e: e.tensor_copy(out=fqsl[0:1, 0:512], in_=nrow[0:1, 512:1024]),
                waits=wfq)
            vev = b.cnt("ve")
            b.do("scalar", lambda e, g=g: e.dma_start(out=fq_s[0:1, g * 512:(g + 1) * 512],
                                                      in_=fqsl[0:1, 0:512]),
                 inc="dw", inck=16, waits=[("ve", vev)])
            dw_fq[g] = b.cnt("dw")
        vch(lambda e: e.tensor_scalar(out=cq[0:1, 16:32], in0=cq[0:1, 0:16],
                                      scalar1=mxk[0:1, 0:1], scalar2=-1.0,
                                      op0=ALU.mult, op1=ALU.mult))
        vch(lambda e: e.tensor_copy(out=cqb_bf[0:1, 0:16], in_=cq[0:1, 16:32]))
        if DBG and bi == DBGB:
            b.wait("gpsimd", "dw", b.cnt("dw"))
            dma("gpsimd", dbg["d_qt"][:, :], qts[:, :], "dg")
            dma("gpsimd", dbg["d_fq"][:, :], fq_s[:, :], "dg")
            b.do("scalar", lambda e: e.dma_start(out=dbg["d_cq"][:, :], in_=cq[:, :]),
                 inc="dw", inck=16, waits=[("ve", b.cnt("ve"))])

    def bisect_init():
        vop(lambda e: e.memset(th, 0.5))
        vop(lambda e: e.memset(lo_, 0.0))
        vop(lambda e: e.memset(hi_, 1.001))

    def bisect_iter():
        vch(lambda e: e.tensor_scalar(
            out=bscr[:, :], in0=Sm8[:, 0:512],
            scalar1=th, scalar2=0.0, op0=ALU.is_ge, op1=ALU.add,
            accum_out=ccols[:, 0:1]))
        for c in range(1, 8):
            vop(lambda e, c=c: e.tensor_scalar(
                out=bscr[:, :], in0=Sm8[:, c * 512:(c + 1) * 512],
                scalar1=th, scalar2=0.0, op0=ALU.is_ge, op1=ALU.add,
                accum_out=ccols[:, c:c + 1]))
        vch(lambda e: e.reduce_sum(out=cntc, in_=ccols, axis=AX.X))
        vch(lambda e: e.tensor_scalar(out=mvec, in0=cntc, scalar1=512.0, scalar2=0.0,
                                      op0=ALU.is_ge, op1=ALU.add))
        vch(lambda e: e.tensor_scalar(out=mivec, in0=mvec, scalar1=-1.0, scalar2=1.0,
                                      op0=ALU.mult, op1=ALU.add))
        vch(lambda e: e.tensor_sub(dvec, th, lo_))
        vch(lambda e: e.tensor_mul(dvec, dvec, mvec))
        vch(lambda e: e.tensor_add(lo_, lo_, dvec))
        vch(lambda e: e.tensor_sub(dvec, th, hi_))
        vch(lambda e: e.tensor_mul(dvec, dvec, mivec))
        vch(lambda e: e.tensor_add(hi_, hi_, dvec))
        vch(lambda e: e.tensor_add(th, lo_, hi_))
        vch(lambda e: e.tensor_scalar_mul(th, th, 0.5))

    def phase_p2(bi):
        bisect_init()
        ve_fq = {}
        se_sw2d = {}
        ve_rsd = {}
        se_exp = {}
        ve_cp = {}
        se_abs = {}
        pe_sc = {}
        pe_sw2 = {}
        dw_at = {}
        pe_fqmm = {}

        def emit_fqcneg(qb):
            wl_ = [("pe", pe_fqmm[qb - 2])] if qb >= 2 else []
            dma("sync", fqsl[0:1, (qb % 2) * 256:(qb % 2) * 256 + 256],
                fq_s[0:1, qb * 256:(qb + 1) * 256], "dx", waits=wl_)
            dxv = b.cnt("dx")
            w_ = [("dx", dxv)]
            if qb >= 2:
                w_.append(("se", se_sw2d[qb - 2]))
            pe_fqmm[qb] = mm(pa[0][:, 256:512], ones1[0:1, :],
                             fqsl[0:1, (qb % 2) * 256:(qb % 2) * 256 + 256], True, True,
                             inc="pe", waits=w_)
            mm(pa[0][:, 0:1], ones1[0:1, :], cqb_bf[0:1, qb:qb + 1], True, True, inc="pe")
            pv = b.cnt("pe")
            vop(lambda e: e.tensor_copy(out=fqbb[:, :], in_=pa[0][:, 256:512]),
                waits=[("pe", pv)])
            w2_ = [("se", se_exp[qb - 2])] if qb >= 2 else []
            vop(lambda e, qb=qb: e.tensor_copy(out=cneg[qb % 2], in_=pa[0][:, 0:1]),
                waits=w2_)
            ve_fq[qb] = b.cnt("ve")

        def emit_scores(qb):
            w_ = [("pe", pe_sc[qb - 2])] if qb >= 2 else []
            b.do("sync", lambda e, qb=qb: e.dma_start(
                out=qsl[qb % 2][:, 0:2048],
                in_=bass.AP(qts, qb * 256, [[4096, 128], [128 * 4096, 8], [1, 256]])),
                inc="dx", inck=16, waits=w_)
            dxq = b.cnt("dx")
            for kt in range(NKT):
                bank = pb[kt % 2]
                for i in range(8):
                    w_ = []
                    if i == 0:
                        if kt < 2:
                            w_.append(("dx", dxq))
                            if qb >= 1:
                                w_.append(("ve", ve_cp[(qb - 1, 30 + kt)]))
                        else:
                            w_.append(("ve", ve_cp[(qb, kt - 2)]))
                    mm(bank[:, 0:256], Kt(i)[:, kt * 128:(kt + 1) * 128],
                       qsl[qb % 2][:, i * 256:(i + 1) * 256],
                       i == 0, i == 7, inc="pe" if i == 7 else None, waits=w_)
                pv = b.cnt("pe")
                if kt == NKT - 1:
                    pe_sc[qb] = pv
                w2_ = [("pe", pv)]
                if qb >= 1 and kt < 2:
                    w2_.append(("se", se_abs[qb - 1]))
                vop(lambda e, kt=kt, bank=bank: e.tensor_copy(out=Bbuf(kt), in_=bank[:, 0:256]),
                    waits=w2_)
                ve_cp[(qb, kt)] = b.cnt("ve")

        def emit_batches(qb):
            for kt in range(NKT):
                w_ = [("ve", ve_cp[(qb, kt)])]
                if kt == 0 and qb >= 1:
                    w_.append(("pe", pe_sw2[qb - 1]))
                act(Abuf(kt), Bbuf(kt), AF.Abs, waits=w_)
            se_abs[qb] = b.cnt("se")
            for kt in range(NKT):
                w_ = [("se", se_abs[qb]), ("ve", ve_fq[qb])] if kt == 0 else []
                vop(lambda e, kt=kt: e.tensor_mul(Abuf(kt), Abuf(kt), fqbb[:, 0:256]),
                    waits=w_)
            ve_t1 = b.cnt("ve")
            for kt in range(NKT):
                w_ = [("ve", ve_t1)] if kt == 0 else []
                act(Abuf(kt), Abuf(kt), AF.Sin, scale=ickc[:, kt:kt + 1], bias=halfpi,
                    waits=w_)
            se_sin = b.cnt("se")
            for kt in range(NKT):
                w_ = [("se", se_sin)] if kt == 0 else []
                vop(lambda e, kt=kt: e.tensor_mul(Abuf(kt), Abuf(kt), Bbuf(kt)), waits=w_)
            ve_w = b.cnt("ve")
            for kt in range(NKT):
                w_ = [("ve", ve_w)] if kt == 0 else []
                act(Abuf(kt), Abuf(kt), AF.Exp, bias=cneg[qb % 2], waits=w_)
            se_exp[qb] = b.cnt("se")

        def emit_sweeps(qb):
            for kt in range(NKT):
                w_ = []
                if kt == 0:
                    w_ = [("se", se_exp[qb])]
                    if qb >= 1:
                        w_ += [("se", se_sw2d[qb - 1]), ("ve", ve_rsd[qb - 1])]
                    if (qb + 1) in ve_fq:
                        w_.append(("ve", ve_fq[qb + 1]))
                for dc in range(4):
                    mm(pa[dc][:, 0:256], Vt(kt)[:, dc * 128:(dc + 1) * 128], Abuf(kt),
                       kt == 0, kt == NKT - 1, waits=(w_ if dc == 0 else ()))
                mm(pr[:, 0:256], vones[:, :], Abuf(kt), kt == 0, kt == NKT - 1,
                   inc="pe" if kt == NKT - 1 else None)
            pv = b.cnt("pe")
            for dc in range(4):
                w_ = [("pe", pv)] if dc == 0 else []
                if dc == 0 and qb >= 1:
                    w_.append(("dw", dw_at[qb - 1]))
                act(atile(dc), pa[dc][:, 0:256], AF.Copy, waits=w_)
            se_d1 = b.cnt("se")
            vop(lambda e: e.tensor_copy(out=atile(8)[0:1, 0:256], in_=pr[0:1, 0:256]),
                waits=[("pe", pv)])
            vev = b.cnt("ve")
            b.do("scalar", lambda e, qb=qb: e.dma_start(
                out=rs_s[0:1, qb * 256:(qb + 1) * 256], in_=atile(8)[0:1, 0:256]),
                inc="dw", inck=16, waits=[("ve", vev)])
            ve_rsd[qb] = b.cnt("ve")
            for kt in range(NKT):
                w_ = [("se", se_d1)] if kt == 0 else []
                for dc in range(4):
                    mm(pa[dc][:, 0:256], Vt(kt)[:, (4 + dc) * 128:(5 + dc) * 128], Abuf(kt),
                       kt == 0, kt == NKT - 1,
                       inc="pe" if (kt == NKT - 1 and dc == 3) else None,
                       waits=(w_ if dc == 0 else ()))
            pe_sw2[qb] = b.cnt("pe")
            pv = pe_sw2[qb]
            for dc in range(4):
                act(atile(4 + dc), pa[dc][:, 0:256], AF.Copy,
                    waits=([("pe", pv)] if dc == 0 else []))
            se_sw2d[qb] = b.cnt("se")
            for dc in range(8):
                b.do("scalar", lambda e, dc=dc, qb=qb: e.dma_start(
                    out=attnT_s[dc * 128:(dc + 1) * 128, qb * 256:(qb + 1) * 256],
                    in_=atile(dc)), inc="dw", inck=16)
            dw_at[qb] = b.cnt("dw")

        emit_fqcneg(0)
        for qb in range(NQB):
            emit_scores(qb)
            if qb >= 1:
                emit_sweeps(qb - 1)
            if qb + 1 < NQB:
                emit_fqcneg(qb + 1)
            emit_batches(qb)
            for _ in range(BIS_PER_QB):
                bisect_iter()
            for z in range(4):
                r0 = (qb * 4 + z) * 64
                b.do("scalar", lambda e, r0=r0: e.dma_start(
                    out=bass.AP(acc_s, r0 * D, [[512, 128], [1, 512]]), in_=zerot),
                    inc="dw", inck=16)
        emit_sweeps(NQB - 1)
        if DBG and bi == DBGB:
            b.wait("gpsimd", "dw", b.cnt("dw"))
            dma("gpsimd", dbg["d_at"][:, :], attnT_s[:, :], "dg")
            dma("gpsimd", dbg["d_rs"][:, :], rs_s[:, :], "dg")

    def phase_p25(bi):
        for _ in range(BIS_PER_QB * NQB, 52):
            bisect_iter()
        vch(lambda e: e.tensor_scalar(
            out=bscr[:, :], in0=Sm8[:, 0:512],
            scalar1=lo_, scalar2=0.0, op0=ALU.is_gt, op1=ALU.add,
            accum_out=ccols[:, 0:1]))
        for c in range(1, 8):
            vop(lambda e, c=c: e.tensor_scalar(
                out=bscr[:, :], in0=Sm8[:, c * 512:(c + 1) * 512],
                scalar1=lo_, scalar2=0.0, op0=ALU.is_gt, op1=ALU.add,
                accum_out=ccols[:, c:c + 1]))
        vch(lambda e: e.reduce_sum(out=ngt, in_=ccols, axis=AX.X))
        vch(lambda e: e.tensor_scalar(out=needc, in0=ngt, scalar1=-1.0, scalar2=512.0,
                                      op0=ALU.mult, op1=ALU.add))
        vch(lambda e: e.tensor_scalar(out=eqm, in0=Sm8[:, :], scalar1=lo_, scalar2=None,
                                      op0=ALU.is_equal))
        vch(lambda e: e.tensor_scalar(out=gtm, in0=Sm8[:, :], scalar1=lo_, scalar2=None,
                                      op0=ALU.is_gt))
        eqrank = AB0[:, 0:8192].bitcast(F32)[0:NE, 0:4096]
        rank8 = AB0[:, 8192:16384].bitcast(F32)[0:NE, 0:4096]
        vch(lambda e: e.tensor_tensor_scan(out=eqrank, data0=eqm, data1=eqm,
                                           initial=0.0, op0=ALU.add, op1=ALU.bypass))
        vch(lambda e: e.scalar_tensor_tensor(out=eqm, in0=eqrank, scalar=needc,
                                             in1=eqm, op0=ALU.is_le, op1=ALU.mult))
        vch(lambda e: e.tensor_add(gtm, gtm, eqm))
        vch(lambda e: e.tensor_tensor_scan(out=rank8, data0=gtm, data1=gtm,
                                           initial=0.0, op0=ALU.add, op1=ALU.bypass))
        vch(lambda e: e.tensor_mul(rank8, rank8, gtm))
        vev = b.cnt("ve")
        for grp in range(4):
            w_ = [("ve", vev if grp == 0 else b.cnt("ve"))]
            for t8 in range(8):
                t = grp * 8 + t8
                tp(pr[0:128, t8 * 8:(t8 + 1) * 8], rank8[0:8, t * 128:(t + 1) * 128],
                   eyef[:, :], inc="pe" if t8 == 7 else None,
                   waits=(w_ if t8 == 0 else ()))
            pv = b.cnt("pe")
            vop(lambda e, grp=grp: e.tensor_copy(
                out=mrankT[:, grp * 64:(grp + 1) * 64], in_=pr[0:128, 0:64]),
                waits=[("pe", pv)])
        b.barrier()
        for d_ in range(8):
            dma("sync", WoTt(d_), wobf[0:1024, d_ * 128:(d_ + 1) * 128], "dx", transpose=True)
        dxWo = b.cnt("dx")
        dma("sync", rsld, rs_s[0:1, :], "dx")
        dxrs = b.cnt("dx")
        pt2 = pt[:, 0:16].rearrange("p (a b) -> p a b", b=2)
        for grp in range(4):
            w_ = [("dx", dxrs)] if grp == 0 else [("ve", b.cnt("ve"))]
            for c8_ in range(8):
                c = grp * 8 + c8_
                tp(pt[0:128, 2 * c8_:2 * c8_ + 1], rsld[0:1, c * 128:(c + 1) * 128],
                   eyeb[:, :], inc="pe" if c8_ == 7 else None,
                   waits=(w_ if c8_ == 0 else ()))
            pv = b.cnt("pe")
            vop(lambda e, grp=grp: e.tensor_copy(out=irsc[:, grp * 8:(grp + 1) * 8],
                                                 in_=pt2[:, :, 0:1]),
                waits=[("pe", pv)])
        vch(lambda e: e.reciprocal(out=irsc[:, :], in_=irsc[:, :]))
        ve_irs = b.cnt("ve")
        b.do("gpsimd", lambda e: e.iota(jiota, pattern=[[1, 512]], base=1,
                                        channel_multiplier=0,
                                        allow_small_or_imprecise_dtypes=True), inc="ge")
        dma("gpsimd", packed, h_hilo[:, :], "dg")
        ge_i = b.cnt("ge"); dg_p = b.cnt("dg")
        dma("sync", AFa[0:1, 1024:2048], bo_d[0:1, :], "dx")
        vop(lambda e: e.tensor_copy(out=bo_row, in_=AFa[0:1, 1024:2048]),
            waits=[("dx", b.cnt("dx"))])
        smt3 = smt_bf.rearrange("p (t e) -> p t e", e=8)
        ve_ig = 0
        for ei in range(NE):
            w_ = [("dg", dg_p)]
            if ei > 0:
                w_.append(("pe", b.cnt("pe")))
            vop(lambda e, ei=ei: e.tensor_copy(out=packed3[:, :, 64:65],
                                               in_=smt3[:, :, ei:ei + 1]),
                waits=w_)
            ve_pk = b.cnt("ve")
            pe_sel = {}
            for t in range(NKT):
                w2_ = [("ge", ge_i), ("ve", ve_pk)] if t < 2 else [("pe", pe_sel[t - 2])]
                vop(lambda e, t=t, ei=ei: e.tensor_scalar(
                    out=selT[t % 2], in0=jiota, scalar1=mrankT[:, t * 8 + ei:t * 8 + ei + 1],
                    scalar2=None, op0=ALU.is_equal), waits=w2_)
                vev = b.cnt("ve")
                w3_ = [("ve", vev)]
                if t == 0 and ei > 0:
                    w3_.append(("ve", ve_ig))
                pe_sel[t] = mm(pr[0:96, 0:512], packed[:, 96 * t:96 * (t + 1)], selT[t % 2],
                               t == 0, t == NKT - 1, inc="pe", waits=w3_)
            pv = b.cnt("pe")
            vop(lambda e: e.tensor_copy(out=igs, in_=pr[0:96, 0:512]), waits=[("pe", pv)])
            ve_ig = b.cnt("ve")
            for c in range(4):
                tp(pr[0:128, c:c + 1], igs[0:1, c * 128:(c + 1) * 128], eyef[:, :],
                   waits=([("ve", ve_ig)] if c == 0 else ()))
            for c in range(4):
                tp(pr[0:128, 4 + c:5 + c], igs[32:33, c * 128:(c + 1) * 128], eyef[:, :],
                   bp=32)
            for c in range(4):
                tp(pr[0:128, 8 + c:9 + c], igs[64:65, c * 128:(c + 1) * 128], eyef[:, :],
                   bp=64, inc="pe" if c == 3 else None)
            pv = b.cnt("pe")
            vop(lambda e: e.tensor_copy(out=stats[:, 0:12], in_=pr[0:128, 0:12]),
                waits=[("pe", pv)])
            vch(lambda e, ei=ei: e.tensor_scalar(
                out=icol_all[:, ei * 4:(ei + 1) * 4], in0=stats[:, 0:4], scalar1=256.0,
                scalar2=None, op0=ALU.mult, accum_out=None) if False else
                e.scalar_tensor_tensor(out=icol_all[:, ei * 4:(ei + 1) * 4],
                                       in0=stats[:, 0:4], scalar=256.0,
                                       in1=stats[:, 4:8], op0=ALU.mult, op1=ALU.add))
            vch(lambda e, ei=ei: e.tensor_copy(out=gcol_all[:, ei * 4:(ei + 1) * 4],
                                               in_=stats[:, 8:12]))
            ve_ig = b.cnt("ve")
        pe_op = {}
        se_yat = {}
        dw_ya = {}
        for qb in range(NQB):
            s_ = qb % 2
            for dc in range(8):
                w_ = []
                if dc == 0:
                    w_ = [("dx", dxWo)]
                    if qb >= 2:
                        w_.append(("pe", pe_op[(qb - 2, 3)]))
                dma("sync", asl(dc, s_),
                    attnT_s[dc * 128:(dc + 1) * 128, qb * 256:(qb + 1) * 256], "dx", waits=w_)
            dxa = b.cnt("dx")
            for qc in range(2):
                for ih in range(2):
                    k_idx = qc * 2 + ih
                    bank = pb[k_idx % 2]
                    for dc in range(8):
                        w2_ = []
                        if dc == 0:
                            w2_ = [("dx", dxa)]
                            sev = se_yat.get((qb, k_idx - 2)) if k_idx >= 2 \
                                else se_yat.get((qb - 1, k_idx + 2))
                            if sev:
                                w2_.append(("se", sev))
                        mm(bank[:, 0:512], asl(dc, s_)[:, qc * 128:(qc + 1) * 128],
                           WoTt(dc)[:, ih * 512:(ih + 1) * 512], dc == 0, False, waits=w2_)
                    mm(bank[:, 0:512],
                       rsld[0:1, qb * 256 + qc * 128:qb * 256 + (qc + 1) * 128],
                       bo_row[0:1, ih * 512:(ih + 1) * 512], False, True, inc="pe")
                    pe_op[(qb, k_idx)] = b.cnt("pe")
                    w3_ = [("pe", pe_op[(qb, k_idx)])]
                    if qb == 0 and k_idx == 0:
                        w3_.append(("ve", ve_irs))
                    if ih == 0:
                        pdw = dw_ya.get((qb - 1, qc))
                        if pdw:
                            w3_.append(("dw", pdw))
                    act(yat[qc][:, ih * 512:(ih + 1) * 512], bank[:, 0:512], AF.Identity,
                        scale=irsc[:, qb * 2 + qc:qb * 2 + qc + 1], waits=w3_)
                    se_yat[(qb, k_idx)] = b.cnt("se")
                b.do("scalar", lambda e, qb=qb, qc=qc: e.dma_start(
                    out=yattn_s[qb * 256 + qc * 128:qb * 256 + (qc + 1) * 128, :],
                    in_=yat[qc]), inc="dw", inck=16)
                dw_ya[(qb, qc)] = b.cnt("dw")
        if DBG and bi == DBGB:
            b.wait("gpsimd", "dw", b.cnt("dw"))
            dma("gpsimd", dbg["d_ya"][:, :], yattn_s[:, :], "dg")
            b.wait("scalar", "ve", b.cnt("ve"))
            b.do("scalar", lambda e: e.dma_start(out=dbg["d_ic"][:, :],
                                                 in_=icol_all[:, :].bitcast(F32)),
                 inc="dw", inck=16)
            b.do("scalar", lambda e: e.dma_start(out=dbg["d_gc"][:, :], in_=gcol_all[:, :]),
                 inc="dw", inck=16)
            b.do("scalar", lambda e: e.dma_start(out=dbg["d_mr"][:, :], in_=mrankT[:, :]),
                 inc="dw", inck=16)

    def phase_p3(bi):
        dg_w1 = {}
        dg_w2 = {}
        dg_b2 = {}
        pe_mm1 = {}
        pe_mm2 = {}
        se_gelu = {}
        se_od = {}
        dg_scat = {}
        pe_ext = {}
        se_ext = {}
        vop(lambda e: e.memset(AB2[:, 0:16384], 0.0))
        vop(lambda e: e.memset(AB3[:, 0:8192], 0.0))
        vop(lambda e: e.memset(AFa[:, 0:2048], 0.0))
        ve_clr = b.cnt("ve")
        b.wait("scalar", "ve", ve_clr)
        b.wait("gpsimd", "ve", ve_clr)
        b.wait("tensor", "ve", ve_clr)
        for h in range(8):
            load_wtile_128x("gpsimd", W1t(h), w1_d, h * 128, 128, "dg", ncols=FF)
        dg_w1[0] = b.cnt("dg")
        for ei in range(NE):
            w_ = [("pe", pe_mm2[ei - 1])] if ei > 0 else []
            b.do("gpsimd", lambda e, ei=ei: e.dma_start(out=b2row, in_=b2_d[ei:ei + 1, :]),
                 inc="dg", inck=16, waits=w_)
            dg_b2[ei] = b.cnt("dg")
            for j in range(4):
                w_ = []
                if j >= 2:
                    w_.append(("pe", pe_ext[(ei, j - 2)]))
                elif ei > 0:
                    w_.append(("pe", pe_ext[(ei - 1, j + 2)]))
                b.do("gpsimd", lambda e, ei=ei, j=j: e.indirect_dma_start(
                    out=exbuf[j % 2], out_offset=None, in_=xn_s[0:128, :],
                    in_offset=bass.IndirectOffsetOnAxis(
                        icol_all[:, ei * 4 + j:ei * 4 + j + 1], 0),
                    bounds_check=bc_reg, oob_is_err=False),
                    inc="dg", inck=16, waits=w_)
                dgv = gfence()
                for c in range(8):
                    w2_ = []
                    if c == 0:
                        w2_ = [("dg", dgv)]
                        sev = se_ext.get((ei, j - 1)) or se_ext.get((ei - 1, 3))
                        if sev:
                            w2_.append(("se", sev))
                    pe_ext[(ei, j)] = tp(pt[:, c * 128:(c + 1) * 128],
                                         exbuf[j % 2][:, c * 128:(c + 1) * 128], eyeb[:, :],
                                         inc="pe", waits=w2_)
                pv = b.cnt("pe")
                for c in range(8):
                    b.do("scalar", lambda e, c=c, j=j: e.copy(
                        out=exTt(c)[:, j * 128:(j + 1) * 128], in_=pt[:, c * 128:(c + 1) * 128]),
                        inc="se", waits=([("pe", pv)] if c == 0 else ()))
                se_ext[(ei, j)] = b.cnt("se")
            for fc in range(32):
                bank = pb[fc % 2]
                for h in range(8):
                    w_ = []
                    if h == 0:
                        w_ = [("dg", dg_w1[ei]), ("se", se_ext[(ei, 3)])]
                        if fc >= 2:
                            w_.append(("se", se_gelu[(ei, fc - 2)]))
                        elif ei > 0:
                            w_.append(("se", se_od[(ei - 1, 7)]))
                    mm(bank[:, 0:512], W1t(h)[:, fc * 128:(fc + 1) * 128], exTt(h),
                       h == 0, h == 7, inc="pe" if h == 7 else None, waits=w_)
                pe_mm1[ei] = b.cnt("pe")
                w2_ = [("pe", pe_mm1[ei])]
                if fc == 0 and ei > 0:
                    w2_.append(("pe", pe_mm2[ei - 1]))
                act(h1T(fc), bank[:, 0:512], AF.Gelu_apprx_tanh,
                    bias=b1c[:, ei * 32 + fc:ei * 32 + fc + 1], waits=w2_)
                se_gelu[(ei, fc)] = b.cnt("se")
            for f in range(8):
                w_ = [("pe", pe_mm2[ei - 1])] if (f == 0 and ei > 0) else []
                load_wtile_128x("gpsimd", AB1[:, f * 4096:(f + 1) * 4096], w2_d,
                                ei * FF + f * 512, 512, "dg", waits=w_)
            dg_w2[ei] = b.cnt("dg")
            if ei + 1 < NE:
                for h in range(8):
                    w_ = [("pe", pe_mm1[ei])] if h == 0 else []
                    load_wtile_128x("gpsimd", W1t(h), w1_d, (ei + 1) * D + h * 128, 128,
                                    "dg", waits=w_, ncols=FF)
                dg_w1[ei + 1] = b.cnt("dg")
            for jc in range(4):
                for ih in range(2):
                    k_idx = jc * 2 + ih
                    bank = pb[k_idx % 2]
                    for fc in range(32):
                        w_ = []
                        if fc == 0:
                            w_ = [("dg", dg_w2[ei]), ("se", se_gelu[(ei, 31)]),
                                  ("dg", dg_b2[ei])]
                            if k_idx >= 2:
                                w_.append(("se", se_od[(ei, k_idx - 2)]))
                        mm(bank[:, 0:512], h1T(fc)[:, jc * 128:(jc + 1) * 128],
                           W2t(fc)[:, ih * 512:(ih + 1) * 512], fc == 0, False, waits=w_)
                    mm(bank[:, 0:512], ones1[0:1, :], b2row[0:1, ih * 512:(ih + 1) * 512],
                       False, True, inc="pe")
                    pv = b.cnt("pe")
                    if k_idx == 7:
                        pe_mm2[ei] = pv
                    w2_ = [("pe", pv)]
                    if ih == 0:
                        psc = dg_scat.get((ei, jc - 2)) if jc >= 2 else dg_scat.get((ei - 1, jc + 2))
                        if psc:
                            w2_.append(("dg", psc))
                    act(out_e[jc % 2][:, ih * 512:(ih + 1) * 512], bank[:, 0:512], AF.Identity,
                        scale=gcol_all[:, ei * 4 + jc:ei * 4 + jc + 1], waits=w2_)
                    se_od[(ei, k_idx)] = b.cnt("se")
                sev = b.cnt("se")
                w3_ = [("se", sev)]
                psc = dg_scat.get((ei, jc - 1)) or dg_scat.get((ei - 1, 3))
                if psc:
                    w3_.append(("dg", psc))
                b.do("gpsimd", lambda e, ei=ei, jc=jc: e.indirect_dma_start(
                    out=acc_s[0:128, :],
                    out_offset=bass.IndirectOffsetOnAxis(
                        icol_all[:, ei * 4 + jc:ei * 4 + jc + 1], 0),
                    in_=out_e[jc % 2][:, 0:1024], in_offset=None,
                    compute_op=ALU.add, bounds_check=bc_reg, oob_is_err=False),
                    inc="dg", inck=16, waits=w3_)
                dg_scat[(ei, jc)] = gfence()
        if DBG and bi == DBGB:
            b.wait("gpsimd", "dg", b.cnt("dg"))
            dma("gpsimd", dbg["d_acc"][:, :], acc_s[:, :], "dg")

    def phase_p4(bi):
        xoff = bi * S
        dw_y = {}
        for t in range(NKT):
            sl = t % 2
            w_ = [("dw", dw_y[t - 2])] if t >= 2 else []
            dma("sync", xa4[sl], x_in[xoff + t * 128:xoff + (t + 1) * 128, :], "dx", waits=w_)
            dma("sync", ac4[sl], acc_s[t * 128:(t + 1) * 128, :], "dx")
            dma("sync", yld[sl], yattn_s[t * 128:(t + 1) * 128, :], "dx")
            dxv = b.cnt("dx")
            vop(lambda e, sl=sl: e.tensor_copy(out=cv4[sl], in_=yld[sl]),
                waits=[("dx", dxv)])
            vop(lambda e, sl=sl: e.tensor_add(xa4[sl], xa4[sl], ac4[sl]))
            vop(lambda e, sl=sl: e.tensor_add(xa4[sl], xa4[sl], cv4[sl]))
            vev = b.cnt("ve")
            b.do("scalar", lambda e, t=t, sl=sl, xo=xoff: e.dma_start(
                out=y_out[xo + t * 128:xo + (t + 1) * 128, :], in_=xa4[sl]),
                inc="dw", inck=16, waits=[("ve", vev)])
            dw_y[t] = b.cnt("dw")

    nbatches = NB if (LVL >= 99 or DBGB == 1) else 1
    for bi in range(nbatches):
        lvl_here = 99 if bi < DBGB else LVL
        phase_p0a(bi)
        b.barrier()
        if lvl_here < 2:
            break
        phase_proj_kv(bi)
        b.barrier()
        if lvl_here < 3:
            break
        phase_p1(bi)
        b.barrier()
        if lvl_here < 4:
            break
        phase_p2(bi)
        b.barrier()
        if lvl_here < 5:
            break
        phase_p25(bi)
        b.barrier()
        if lvl_here < 6:
            break
        phase_p3(bi)
        b.barrier()
        if lvl_here < 7:
            break
        phase_p4(bi)
        b.barrier()

    b.wait("sync", "dw", b.cnt("dw"))
    b.wait("sync", "dg", b.cnt("dg"))
    b.wait("sync", "dx", b.cnt("dx"))

    with es:
        with nc.Block() as block:
            @block.tensor
            def _(e):
                for fn in b.streams["tensor"]:
                    fn(e)

            @block.vector
            def _(e):
                for fn in b.streams["vector"]:
                    fn(e)

            @block.scalar
            def _(e):
                for fn in b.streams["scalar"]:
                    fn(e)

            @block.gpsimd
            def _(e):
                for fn in b.streams["gpsimd"]:
                    fn(e)

            @block.sync
            def _(e):
                for fn in b.streams["sync"]:
                    fn(e)
    return nc


def make_in_maps(x, Wq, bq, Wk, bk, Wv, bv, Wo, bo, Wg, W1, b1, W2, b2):
    common = {
        "wq": np.ascontiguousarray(np.asarray(Wq, np.float32)),
        "wk": np.ascontiguousarray(np.asarray(Wk, np.float32)),
        "wv": np.ascontiguousarray(np.asarray(Wv, np.float32)),
        "wo": np.ascontiguousarray(np.asarray(Wo, np.float32)),
        "bq": np.ascontiguousarray(np.asarray(bq, np.float32).reshape(1, D)),
        "bk": np.ascontiguousarray(np.asarray(bk, np.float32).reshape(1, D)),
        "bv": np.ascontiguousarray(np.asarray(bv, np.float32).reshape(1, D)),
        "bo": np.ascontiguousarray(np.asarray(bo, np.float32).reshape(1, D)),
        "wg": np.ascontiguousarray(np.asarray(Wg, np.float32)),
        "w1": np.ascontiguousarray(np.asarray(W1, np.float32).reshape(NE * D, FF)),
        "b1": np.ascontiguousarray(np.asarray(b1, np.float32)),
        "w2": np.ascontiguousarray(np.asarray(W2, np.float32).reshape(NE * FF, D)),
        "b2": np.ascontiguousarray(np.asarray(b2, np.float32)),
    }
    x = np.asarray(x, np.float32)
    in_maps = []
    for c in range(8):
        m = dict(common)
        m["x"] = np.ascontiguousarray(x[c * NB:(c + 1) * NB].reshape(NB * S, D))
        in_maps.append(m)
    return in_maps


def kernel(L0_attn_x, L0_attn_Wq, L0_attn_bq, L0_attn_Wk, L0_attn_bk,
           L0_attn_Wv, L0_attn_bv, L0_attn_Wo, L0_attn_bo,
           L0_ffn_Wg, L0_ffn_W1, L0_ffn_b1, L0_ffn_W2, L0_ffn_b2):
    nc = build_program("full")
    in_maps = make_in_maps(L0_attn_x, L0_attn_Wq, L0_attn_bq, L0_attn_Wk, L0_attn_bk,
                           L0_attn_Wv, L0_attn_bv, L0_attn_Wo, L0_attn_bo,
                           L0_ffn_Wg, L0_ffn_W1, L0_ffn_b1, L0_ffn_W2, L0_ffn_b2)
    res = run_bass_kernel_spmd(nc, in_maps, core_ids=list(range(8)))
    try:
        import pickle
        with open("/tmp/kdbg.pkl", "wb") as fh:
            pickle.dump({f"c{c}_{k}": np.asarray(v) for c in range(8)
                         for k, v in res.results[c].items() if k != "y"}, fh)
    except Exception as ex:
        print("dbg save failed", ex)
    y = np.stack([np.asarray(res.results[c]["y"]).reshape(NB, S, D) for c in range(8)])
    return y.reshape(16, S, D)
